# version 14
# baseline (speedup 1.0000x reference)
import numpy as np
import ml_dtypes
from contextlib import ExitStack

import concourse.bass as bass
import concourse.mybir as mybir
from concourse.bass_utils import run_bass_kernel_spmd

F32 = mybir.dt.float32
BF16 = mybir.dt.bfloat16
I32 = mybir.dt.int32
U8 = mybir.dt.uint8
AF = mybir.ActivationFunctionType
ALU = mybir.AluOpType

NCORE = 8
D = 2048
T = 2048
NH = 16
HD = 128
BLK = 256
NB = T // BLK
NG = 8
HID = 5632
HCN = 44
TFF = 1024
LN_EPS = 1e-5
RMS_EPS = 1e-6


def _bf16_bits(x):
    return np.ascontiguousarray(x.astype(ml_dtypes.bfloat16).view(np.uint16))


def build(debug=False):
    nc = bass.Bass("TRN2", target_bir_lowering=False)

    xt = nc.dram_tensor("xt", [D, T], F32, kind="ExternalInput").ap()
    wqkv = nc.dram_tensor("wqkv", [NG, 128, 6 * 16 * 128], F32, kind="ExternalInput").ap()
    wg = nc.dram_tensor("wg", [16, 128, 16 * 128], F32, kind="ExternalInput").ap()
    wo = nc.dram_tensor("wo", [16, 128, 16 * 128], F32, kind="ExternalInput").ap()
    fg = nc.dram_tensor("fg", [HCN, 128, 16 * 128], I32, kind="ExternalInput").ap()
    fu = nc.dram_tensor("fu", [HCN, 128, 16 * 128], I32, kind="ExternalInput").ap()
    fd = nc.dram_tensor("fd", [16, 128, HCN * 128], I32, kind="ExternalInput").ap()
    sg = nc.dram_tensor("sg", [128, HCN], F32, kind="ExternalInput").ap()
    su = nc.dram_tensor("su", [128, HCN], F32, kind="ExternalInput").ap()
    sd = nc.dram_tensor("sd", [128, 16], F32, kind="ExternalInput").ap()
    nw = nc.dram_tensor("nw", [1, D], F32, kind="ExternalInput").ap()
    sr = nc.dram_tensor("sr", [1, NH], F32, kind="ExternalInput").ap()
    mrow = nc.dram_tensor("mrow", [1, NCORE], F32, kind="ExternalInput").ap()

    out_d = nc.dram_tensor("out", [D, T], F32, kind="ExternalOutput").ap()

    ikind = "ExternalOutput" if debug else "Internal"
    if debug == "kinds":
        debug = False
        ikind = "ExternalOutput"
    gate_sc = nc.dram_tensor("gate_sc", [D, T], BF16, kind=ikind).ap()
    out_sc = nc.dram_tensor("out_sc", [D, T], BF16, kind=ikind).ap()
    q_sc = nc.dram_tensor("q_sc", [D, T], BF16, kind=ikind).ap()
    g_sc = nc.dram_tensor("g_sc", [D, T], BF16, kind=ikind).ap()
    kv_local = nc.dram_tensor("kv_local", [NH * HD, HD], F32, kind="Internal")
    kv_all = nc.dram_tensor("kv_all", [NCORE * NH * HD, HD], F32, kind="Internal", addr_space="Shared")
    if debug:
        dbg_xn = nc.dram_tensor("dbg_xn", [128, 16 * T], F32, kind="ExternalOutput").ap()
        dbg_rt = nc.dram_tensor("dbg_rt", [1, T], F32, kind="ExternalOutput").ap()
        dbg_kvin = nc.dram_tensor("dbg_kvin", [128, NH * HD], F32, kind="ExternalOutput").ap()

    iden_c = nc.inline_tensor(_bf16_bits(np.eye(128, dtype=np.float32)), name="idenb")
    arr_c = nc.inline_tensor(_bf16_bits(
        np.broadcast_to(np.arange(1, BLK + 1, dtype=np.float32), (128, BLK)).copy()), name="arrrep")
    n_idx = (np.arange(128)[:, None, None] + 128 * np.arange(2)[None, :, None]).astype(np.float32)
    m_idx = np.broadcast_to(np.arange(BLK, dtype=np.float32), (128, 2, BLK))
    idxT_np = np.where(m_idx >= n_idx, m_idx - n_idx, np.float32(1e9)).astype(np.float32)
    idxT_c = nc.inline_tensor(_bf16_bits(np.ascontiguousarray(idxT_np)), name="idxT")
    karr_np = np.broadcast_to(
        (255.0 - (np.arange(128)[:, None, None] + 128 * np.arange(2)[None, :, None])), (128, 2, 128)
    ).astype(np.float32)
    karr_c = nc.inline_tensor(_bf16_bits(np.ascontiguousarray(karr_np)), name="karr")

    with ExitStack() as stk:
        e = stk.enter_context
        iden = e(nc.sbuf_tensor("iden", [128, 128], BF16))
        ones_c = e(nc.sbuf_tensor("ones_c", [128, 1], BF16))
        ones_rf = e(nc.sbuf_tensor("ones_rf", [1, 128], F32))
        n128_r = e(nc.sbuf_tensor("n128_r", [1, 128], BF16))
        srow = e(nc.sbuf_tensor("srow", [1, NH], F32))
        s_rep = e(nc.sbuf_tensor("s_rep", [128, NH], F32))
        neg_s = e(nc.sbuf_tensor("neg_s", [128, NH], F32))
        bd_all = e(nc.sbuf_tensor("bd_all", [128, NH], F32))
        bdg = e(nc.sbuf_tensor("bdg", [128, NB, NH], F32))
        mrep = e(nc.sbuf_tensor("mrep", [128, NCORE], F32))
        scx = e(nc.sbuf_tensor("scx", [128, NH, NCORE], F32))
        sg_sb = e(nc.sbuf_tensor("sg_sb", [128, HCN], F32))
        su_sb = e(nc.sbuf_tensor("su_sb", [128, HCN], F32))
        sd_sb = e(nc.sbuf_tensor("sd_sb", [128, 16], F32))
        arr_sb = e(nc.sbuf_tensor("arr_sb", [128, BLK], BF16))
        idxT_sb = e(nc.sbuf_tensor("idxT_sb", [128, 2, BLK], BF16))
        karr_sb = e(nc.sbuf_tensor("karr_sb", [128, 2, 128], BF16))
        rt_bf = e(nc.sbuf_tensor("rt_bf", [1, T], BF16))
        rh_bf = e(nc.sbuf_tensor("rh_bf", [1, TFF], BF16))
        qdec = e(nc.sbuf_tensor("qdec", [128, 2, BLK], BF16))
        diagT = e(nc.sbuf_tensor("diagT", [128, 2, 2, BLK], BF16))
        kdecrep = e(nc.sbuf_tensor("kdecrep", [128, 2, 2, 128], BF16))
        kvf = e(nc.sbuf_tensor("kvf", [128, 2, HD], F32))
        kvbf = e(nc.sbuf_tensor("kvbf", [128, 2, HD], BF16))
        kvtok = e(nc.sbuf_tensor("kvtok", [128, 2, 4, 128], BF16))
        ktl = e(nc.sbuf_tensor("ktl", [128, 2, 2, 128], BF16))
        qkd = e(nc.sbuf_tensor("qkd", [128, 2, 2, BLK], BF16))
        eps_ln = e(nc.sbuf_tensor("eps_ln", [1, 1], F32))
        kvc = e(nc.sbuf_tensor("kvc", [128, 4, HD], F32))
        eps_rms = e(nc.sbuf_tensor("eps_rms", [1, 1], F32))

        Abuf = e(nc.sbuf_tensor("Abuf", [128, 65536], U8))
        Bbuf = e(nc.sbuf_tensor("Bbuf", [128, 121856], U8))
        Abf = Abuf.bitcast(BF16).ap()

        def xn_ap(dc):
            return Abf[:, dc * T:(dc + 1) * T]
        u_ap = xn_ap

        Bbf = Bbuf.bitcast(BF16).ap()
        Bf = Bbuf.bitcast(F32).ap()

        def bview(dt_ap, byte_off, n):
            esz = 2 if dt_ap is Bbf else 4
            assert byte_off % esz == 0
            return dt_ap[:, byte_off // esz: byte_off // esz + n]

        psA = e(nc.psum_tensor("psA", [128, 2048], F32))
        psB = e(nc.psum_tensor("psB", [128, 2048], F32))

        POOL = {"L": 6, "W": 6, "O": 2}
        sem = {}
        for s in ["P", "V", "S", "C"]:
            sem[s] = e(nc.semaphore(s))
        for q, n in POOL.items():
            for i in range(n):
                sem[f"{q}{i}"] = e(nc.semaphore(f"{q}{i}"))
        cnt = {s: 0 for s in sem}
        pool_idx = {q: 0 for q in POOL}
        OWN = {"P": "tensor", "V": "vector", "S": "scalar"}
        ops = []

        def op(eng, fn, waits=(), inc=None):
            flat = []
            for wt in waits:
                if wt is None:
                    continue
                if isinstance(wt, list):
                    flat.extend(x for x in wt if x is not None)
                else:
                    flat.append(wt)
            if inc is not None and inc[0] in POOL:
                q = inc[0]
                i = pool_idx[q]
                pool_idx[q] = (i + 1) % POOL[q]
                name = f"{q}{i}"
                if cnt[name] > 0:
                    flat.append((name, cnt[name]))
                inc = (name, 16)
            ws = []
            for (s, v) in flat:
                if v <= 0 or (s in OWN and OWN[s] == eng):
                    continue
                assert v <= cnt[s], f"wait on future token {s}={v} (now {cnt[s]})"
                if (s, v) not in ws:
                    ws.append((s, v))
            ops.append((eng, ws, fn, inc))
            if inc is not None:
                cnt[inc[0]] += inc[1]
                return (inc[0], cnt[inc[0]])
            return None

        def tok(s):
            return (s, cnt[s])

        def qtoks(q):
            return [(f"{q}{i}", cnt[f"{q}{i}"]) for i in range(POOL[q]) if cnt[f"{q}{i}"] > 0]

        _last_ptok = [None]

        def ptok():
            for i in range(len(ops) - 1, -1, -1):
                if ops[i][0] == "tensor":
                    if ops[i][3] is not None:
                        return _last_ptok[0]
                    ops[i] = (ops[i][0], ops[i][1], ops[i][2], ("P", 1))
                    break
            else:
                raise AssertionError("no tensor op to attach token to")
            cnt["P"] += 1
            _last_ptok[0] = ("P", cnt["P"])
            return _last_ptok[0]

        op("sync", lambda s_: s_.dma_start(out=iden[:, :], in_=iden_c.ap().bitcast(BF16)), inc=("L", 16))
        op("sync", lambda s_: s_.dma_start(out=arr_sb[:, :], in_=arr_c.ap().bitcast(BF16)), inc=("L", 16))
        op("sync", lambda s_: s_.dma_start(out=idxT_sb[:, :, :], in_=idxT_c.ap().bitcast(BF16)), inc=("L", 16))
        op("sync", lambda s_: s_.dma_start(out=karr_sb[:, :, :], in_=karr_c.ap().bitcast(BF16)), inc=("L", 16))
        op("sync", lambda s_: s_.dma_start(out=srow[:, :], in_=sr), inc=("L", 16))
        op("sync", lambda s_: s_.dma_start(out=sg_sb[:, :], in_=sg), inc=("L", 16))
        op("sync", lambda s_: s_.dma_start(out=su_sb[:, :], in_=su), inc=("L", 16))
        op("sync", lambda s_: s_.dma_start(out=sd_sb[:, :], in_=sd), inc=("L", 16))
        op("sync", lambda s_: s_.dma_start(out=mrep[:, :], in_=mrow.broadcast_to([128, NCORE])), inc=("L", 16))
        lConst = qtoks("L")

        op("vector", lambda v: v.memset(ones_c[:, :], 1.0), inc=("V", 1))
        op("vector", lambda v: v.memset(ones_rf[:, :], 1.0), inc=("V", 1))
        op("vector", lambda v: v.memset(n128_r[:, :], -128.0), inc=("V", 1))
        op("vector", lambda v: v.memset(eps_ln[:, :], LN_EPS), inc=("V", 1))
        op("vector", lambda v: v.memset(eps_rms[:, :], RMS_EPS), inc=("V", 1))
        vMemset = tok("V")

        op("tensor", lambda t: t.matmul(psB[:, 0:NH], ones_rf[:, :], srow[:, :], start=True, stop=True),
           waits=[lConst, vMemset])
        pSrep = ptok()
        op("vector", lambda v: v.tensor_copy(s_rep[:, :], psB[:, 0:NH]), waits=[pSrep], inc=("V", 1))
        op("vector", lambda v: v.tensor_scalar_mul(neg_s[:, :], s_rep[:, :], -1.0), inc=("V", 1))
        vSrep = tok("V")
        op("scalar", lambda s_: s_.activation(bd_all[:, :], s_rep[:, :], AF.Exp, scale=-256.0),
           waits=[vSrep], inc=("S", 1))
        for b in range(NB):
            op("scalar", lambda s_, b=b: s_.activation(bdg[:, b, :], s_rep[:, :], AF.Exp, scale=-256.0 * b),
               inc=("S", 1))
        sConsts = tok("S")

        XBF = bview(Bbf, 0, 16 * T).rearrange("p (dc t) -> p dc t", t=T)
        SQ = [bview(Bbf, 65536, T), bview(Bbf, 69632, T)]
        row_mean = bview(Bf, 73728, T)[0:1, :]
        row_b = bview(Bf, 81920, T)[0:1, :]
        tmpP1 = bview(Bf, 90112, T)
        std_row = tmpP1[0:1, :]
        rstd_rep = bview(Bf, 98304, T)
        nm_rep = bview(Bf, 106496, T)

        xbf_dma = []
        for i in range(8):
            xbf_dma.append(op("gpsimd", lambda g, i=i: g.dma_start(
                out=XBF[:, 2 * i:2 * i + 2, :],
                in_=xt.rearrange("(dc p) t -> p dc t", p=128)[:, 2 * i:2 * i + 2, :]), inc=("W", 16)))

        statsTok = []
        for dc in range(16):
            wq_ = [xbf_dma[dc // 2]]
            if dc >= 2:
                wq_.append(statsTok[dc - 2])
            sqT = op("vector", lambda v, dc=dc: v.tensor_mul(SQ[dc % 2][:, :], XBF[:, dc, :], XBF[:, dc, :]),
                     waits=wq_, inc=("V", 1))
            w = [xbf_dma[dc // 2], vMemset, vSrep]
            for j in range(4):
                op("tensor", lambda t, dc=dc, j=j: t.matmul(
                    psB[0:1, j * 512:(j + 1) * 512], ones_c[:, :], XBF[:, dc, j * 512:(j + 1) * 512],
                    start=(dc == 0), stop=(dc == 15)), waits=(w if j == 0 else []))
            for j in range(4):
                op("tensor", lambda t, dc=dc, j=j: t.matmul(
                    psA[0:1, j * 512:(j + 1) * 512], ones_c[:, :], SQ[dc % 2][:, j * 512:(j + 1) * 512],
                    start=(dc == 0), stop=(dc == 15)), waits=([sqT] if j == 0 else []))
            statsTok.append(ptok())
        pStats = statsTok[-1]

        op("scalar", lambda s_: s_.activation(row_mean[:, :], psB[0:1, :], AF.Copy, scale=1.0 / D),
           waits=[pStats], inc=("S", 1))
        op("scalar", lambda s_: s_.activation(row_b[:, :], psB[0:1, :], AF.Square, scale=1.0 / D),
           inc=("S", 1))
        sRows1 = tok("S")
        op("vector", lambda v: v.scalar_tensor_tensor(row_b[:, :], psA[0:1, :], 1.0 / D, row_b[:, :],
                                                      ALU.mult, ALU.subtract), waits=[sRows1], inc=("V", 1))
        vVar = tok("V")
        op("scalar", lambda s_: s_.activation(std_row[:, :], row_b[:, :], AF.Sqrt, bias=eps_ln[:, :]),
           waits=[vVar], inc=("S", 1))
        sStd = tok("S")
        op("vector", lambda v: v.reciprocal(row_b[:, :], std_row[:, :]), waits=[sStd], inc=("V", 1))
        op("vector", lambda v: v.scalar_tensor_tensor(row_mean[:, :], row_mean[:, :], -1.0, row_b[:, :],
                                                      ALU.mult, ALU.mult), inc=("V", 1))
        vRowsDone = tok("V")

        for j in range(4):
            op("tensor", lambda t, j=j: t.matmul(psA[:, j * 512:(j + 1) * 512], ones_rf[:, :],
                                                 row_b[0:1, j * 512:(j + 1) * 512], start=True, stop=True),
               waits=([vRowsDone] if j == 0 else []))
        for j in range(4):
            op("tensor", lambda t, j=j: t.matmul(psB[:, j * 512:(j + 1) * 512], ones_rf[:, :],
                                                 row_mean[0:1, j * 512:(j + 1) * 512], start=True, stop=True))
        pBc = ptok()
        op("vector", lambda v: v.tensor_copy(rstd_rep[:, :], psA[:, :]), waits=[pBc], inc=("V", 1))
        vRstdRep = op("vector", lambda v: v.tensor_copy(nm_rep[:, :], psB[:, :]), inc=("V", 1))
        vNmRep = vRstdRep

        DCO = [6, 7, 8, 9, 10, 11, 12, 13, 14, 15, 0, 1, 2, 3, 4, 5]
        applyTok = {}
        for k, dc in enumerate(DCO):
            op("vector", lambda v, dc=dc: v.tensor_mul(tmpP1[:, :], XBF[:, dc, :], rstd_rep[:, :]),
               waits=([pBc] if k == 0 else []), inc=("V", 1))
            applyTok[dc] = op("vector", lambda v, dc=dc: v.tensor_add(
                xn_ap(dc)[:, :], tmpP1[:, :], nm_rep[:, :]), inc=("V", 1))
        vXnDone = tok("V")

        if debug:
            op("gpsimd", lambda g: g.dma_start(out=dbg_xn, in_=Abf[:, :]), waits=[vXnDone], inc=("W", 16))

        qkv_sb = bview(Bbf, 0, 6 * T).rearrange("p (fc t) -> p fc t", t=T)
        qkvw = [bview(Bbf, 24576, 12288).rearrange("p (fc dc f) -> p fc dc f", fc=6, dc=16),
                bview(Bbf, 49152, 12288).rearrange("p (fc dc f) -> p fc dc f", fc=6, dc=16)]
        out_loc = bview(Bbf, 73728, 2 * T).rearrange("p (hh t) -> p hh t", t=T)
        qt_buf = bview(Bbf, 81920, 2 * T).rearrange("p (hh t) -> p hh t", t=T)

        def psO(hh):
            return psB[:, hh * 1024: hh * 1024 + 256]

        def psKVd(hh):
            return psB[:, hh * 1024 + 256: hh * 1024 + 384]

        def psQK(hh):
            return psB[:, hh * 1024 + 512: hh * 1024 + 1024]

        def psQKbf(hh):
            return psB.ap()[:, hh * 1024 + 512: hh * 1024 + 1024].bitcast(BF16)

        wgbuf = [bview(Bbf, 90112, 2048).rearrange("p (dc f) -> p dc f", dc=16),
                 bview(Bbf, 94208, 2048).rearrange("p (dc f) -> p dc f", dc=16)]
        gatebuf = [bview(Bbf, 98304, T), bview(Bbf, 102400, T)]
        wgDmaT = {}
        gStop = {}
        gStore = {}
        gEvacH2 = {}

        projUseTok = {}
        regTok = {0: None, 1: None}
        lastScanPE = None
        lastQtilV = None
        lastOevac = {0: None, 1: None}
        lastQkd = {0: None, 1: None}
        lastIntra = {0: None, 1: None}
        lastCast = {0: None, 1: None}
        kvStoreTok = []
        outStoreTok = None
        qtStoreTok = None

        vRt = None
        for g in range(NG):
            if g == 1:
                w0 = [vXnDone, lastOevac[0], lastOevac[1], lastQkd[0], lastQkd[1],
                      lastCast[0], lastCast[1]]
                for j in range(4):
                    for dc in range(16):
                        op("tensor", lambda t, j=j, dc=dc: t.matmul(
                            psB[0:1, j * 512:(j + 1) * 512], ones_c[:, :],
                            xn_ap(dc)[:, j * 512:(j + 1) * 512],
                            start=(dc == 0), stop=(dc == 15)),
                            waits=(w0 if (j == 0 and dc == 0) else []))
                pRt = ptok()
                vRt = op("vector", lambda v: v.tensor_copy(rt_bf[:, :], psB[0:1, :]), waits=[pRt], inc=("V", 1))
                if debug:
                    op("gpsimd", lambda g_: g_.dma_start(out=dbg_rt, in_=rt_bf[:, :]), waits=[vRt], inc=("W", 16))
            par = g % 2
            wDma = []
            for i in range(3):
                if g >= 2:
                    w = [projUseTok.get(g - 2)] if i == 0 else []
                elif g == 0:
                    w = [applyTok[min(15, 7 + 2 * i)]]
                else:
                    w = [applyTok[min(15, 13 + 2 * i)]] if i < 2 else [applyTok[5], statsTok[15]]
                wDma.append(op("gpsimd", lambda gp, g=g, i=i, par=par: gp.dma_start(
                    out=qkvw[par][:, 2 * i:2 * i + 2, :, :],
                    in_=wqkv[g].rearrange("p (fc dc f) -> p fc dc f", fc=6, dc=16)[:, 2 * i:2 * i + 2, :, :]),
                    waits=w, inc=("W", 16)))

            decTok = {}
            for hh in range(2):
                h = 2 * g + hh
                op("scalar", lambda s_, hh=hh, h=h: s_.activation(
                    qdec[:, hh, :], arr_sb[:, :], AF.Exp, scale=neg_s[:, h:h + 1]),
                    waits=([lConst, vSrep, sConsts] if hh == 0 else []), inc=("S", 1))
                op("scalar", lambda s_, hh=hh, h=h: s_.activation(
                    diagT[:, hh, :, :], idxT_sb[:, :, :], AF.Exp, scale=neg_s[:, h:h + 1]), inc=("S", 1))
                decTok[hh] = op("scalar", lambda s_, hh=hh, h=h: s_.activation(
                    kdecrep[:, hh, :, :], karr_sb[:, :, :], AF.Exp, scale=neg_s[:, h:h + 1]), inc=("S", 1))

            evacTok = {}
            for fc in range(6):
                for half in range(2):
                    r = (fc * 2 + half) % 2
                    w = [wDma[fc // 2], regTok[r]]
                    dcs = DCO if (g == 0 and fc == 0 and half == 0) else list(range(16))
                    for kdc, dc in enumerate(dcs):
                        for j in range(2):
                            ww = w if (kdc == 0 and j == 0) else []
                            if g == 0 and fc == 0 and half == 0 and j == 0:
                                ww = ww + [applyTok[dc]]
                                if kdc == 0:
                                    ww = ww + [vRstdRep]
                            op("tensor", lambda t, par=par, fc=fc, half=half, dc=dc, j=j, r=r,
                               kdc=kdc: t.matmul(
                                psA[:, r * 1024 + j * 512: r * 1024 + (j + 1) * 512],
                                qkvw[par][:, fc, dc, :],
                                xn_ap(dc)[:, half * 1024 + j * 512: half * 1024 + (j + 1) * 512],
                                start=(kdc == 0), stop=(kdc == 15)),
                                waits=ww)
                    pStop = ptok()
                    we = [pStop]
                    if g > 0 and fc == 0 and half == 0:
                        we += [lastScanPE, lastQtilV]
                    evacTok[(fc, half)] = op("scalar", lambda s_, fc=fc, half=half, r=r: s_.activation(
                        qkv_sb[:, fc, half * 1024:(half + 1) * 1024], psA[:, r * 1024:(r + 1) * 1024], AF.Silu),
                        waits=we, inc=("S", 1))
                    regTok[r] = evacTok[(fc, half)]
            evacQ = {hh: evacTok[(hh, 1)] for hh in range(2)}
            evacK = {hh: evacTok[(2 + hh, 1)] for hh in range(2)}
            evacV = {hh: evacTok[(4 + hh, 1)] for hh in range(2)}

            class _Filler:
                def __init__(self):
                    self.th = []
                    self.i = 0

                def add(self, fn):
                    self.th.append(fn)

                def pump(self, n):
                    stop = min(self.i + n, len(self.th))
                    while self.i < stop:
                        self.th[self.i]()
                        self.i += 1

                def flush(self):
                    self.pump(len(self.th))

            filler = _Filler()
            for oc in (2 * g, 2 * g + 1):
                def _dma(oc=oc):
                    wgDmaT[oc] = op("gpsimd", lambda gp, oc=oc: gp.dma_start(
                        out=wgbuf[oc % 2][:, :, :], in_=wg[oc].rearrange("p (dc f) -> p dc f", dc=16)),
                        waits=([applyTok[DCO[-1]]] if oc < 2 else [gStop[oc - 2]]), inc=("W", 16))
                filler.add(_dma)
                for r in range(2):
                    for dc in range(16):
                        def _mm(oc=oc, r=r, dc=dc):
                            w = [wgDmaT[oc], regTok[r]] if dc == 0 else []
                            for j in range(2):
                                op("tensor", lambda t, oc=oc, dc=dc, j=j, r=r: t.matmul(
                                    psA[:, r * 1024 + j * 512: r * 1024 + (j + 1) * 512],
                                    wgbuf[oc % 2][:, dc, :],
                                    xn_ap(dc)[:, r * 1024 + j * 512: r * 1024 + (j + 1) * 512],
                                    start=(dc == 0), stop=(dc == 15)),
                                    waits=(w if j == 0 else []))
                        filler.add(_mm)

                    def _evac(oc=oc, r=r):
                        pG = ptok()
                        if r == 1:
                            gStop[oc] = pG
                        we = [pG]
                        if r == 0 and oc >= 2:
                            we.append(gStore[oc - 2])
                        regTok[r] = op("scalar", lambda s_, oc=oc, r=r: s_.activation(
                            gatebuf[oc % 2][:, r * 1024:(r + 1) * 1024],
                            psA[:, r * 1024:(r + 1) * 1024], AF.Sigmoid), waits=we, inc=("S", 1))
                        gEvacH2[(oc, r)] = regTok[r]
                    filler.add(_evac)

                def _store(oc=oc):
                    gStore[oc] = op("sync", lambda s_, oc=oc: s_.dma_start(
                        out=gate_sc[oc * 128:(oc + 1) * 128, :], in_=gatebuf[oc % 2][:, :]),
                        waits=[gEvacH2[(oc, 0)], gEvacH2[(oc, 1)], gStore.get(oc - 2)], inc=("L", 16))
                filler.add(_store)

            for b in range(NB):
                c0 = b * BLK
                vq = {}
                vev = {}
                vKtl = {}
                pQk = {}
                pKvd = {}
                for hh in range(2):
                    qs = qkv_sb[:, hh, c0:c0 + BLK]
                    ks = qkv_sb[:, 2 + hh, c0:c0 + BLK]
                    vq[hh] = op("vector", lambda v, hh=hh, qs=qs, c0=c0: v.tensor_mul(
                        qt_buf[:, hh, c0:c0 + BLK], qs, qdec[:, hh, :]),
                        waits=[evacQ[hh], decTok[hh], lastIntra[hh],
                               (qtStoreTok if b == 0 else None)], inc=("V", 1))
                    w = [evacK[hh], evacV[hh], lConst, lastQkd[hh], lastIntra[hh]]
                    if g == 0 and b == 0:
                        w.append(vNmRep)
                    if g == 1 and b == 0:
                        w.append(vRt)
                    for nh in range(2):
                        op("tensor", lambda t, hh=hh, ks=ks, nh=nh: t.transpose(
                            psQKbf(hh)[:, nh * 128:(nh + 1) * 128], ks[:, nh * 128:(nh + 1) * 128], iden[:, :]),
                            waits=(w if nh == 0 else []))
                    for nh in range(2):
                        op("tensor", lambda t, hh=hh, c0=c0, nh=nh: t.transpose(
                            psQKbf(hh)[:, 256 + nh * 128: 256 + (nh + 1) * 128],
                            qkv_sb[:, 4 + hh, c0 + nh * 128: c0 + (nh + 1) * 128], iden[:, :]))
                    pTr = ptok()
                    vev[hh] = op("vector", lambda v, hh=hh: v.tensor_copy(
                        kvtok[:, hh, :, :].rearrange("p a b -> p (a b)"), psQKbf(hh)[:, 0:512]),
                        waits=[pTr], inc=("V", 1))
                    op("vector", lambda v, hh=hh: v.tensor_mul(
                        ktl[:, hh, :, :].rearrange("p a b -> p (a b)"),
                        kvtok[:, hh, 0:2, :].rearrange("p a b -> p (a b)"),
                        kdecrep[:, hh, :, :].rearrange("p a b -> p (a b)")), inc=("V", 1))
                    vKtl[hh] = tok("V")
                filler.pump(1)
                for hh in range(2):
                    w = [vKtl[hh], lastOevac[hh], lastCast[hh]]
                    if b == 0:
                        w.append(kvStoreTok[-1] if kvStoreTok else None)
                    for nh in range(2):
                        op("tensor", lambda t, hh=hh, nh=nh: t.matmul(
                            psKVd(hh), ktl[:, hh, nh, :], kvtok[:, hh, 2 + nh, :],
                            start=(nh == 0), stop=(nh == 1)),
                            waits=(w if nh == 0 else []))
                    pKvd[hh] = ptok()
                for hh in range(2):
                    if b > 0:
                        op("tensor", lambda t, hh=hh, c0=c0: t.matmul(
                            psO(hh), kvbf[:, hh, :], qt_buf[:, hh, c0:c0 + BLK], start=True, stop=False),
                            waits=[lastCast[hh], vq[hh]])
                for hh in range(2):
                    qs = qkv_sb[:, hh, c0:c0 + BLK]
                    ks = qkv_sb[:, 2 + hh, c0:c0 + BLK]
                    for nh in range(2):
                        op("tensor", lambda t, hh=hh, ks=ks, qs=qs, nh=nh: t.matmul(
                            psQK(hh)[:, nh * 256:(nh + 1) * 256], ks[:, nh * 128:(nh + 1) * 128], qs,
                            start=True, stop=True),
                            waits=([vev[hh], vq[hh]] if nh == 0 else []))
                    pQk[hh] = ptok()
                    op("vector", lambda v, hh=hh: v.tensor_tensor(
                        qkd[:, hh, :, :].rearrange("p a b -> p (a b)"), psQK(hh)[:, 0:512],
                        diagT[:, hh, :, :].rearrange("p a b -> p (a b)"),
                        ALU.mult), waits=[pQk[hh]], inc=("V", 1))
                    lastQkd[hh] = tok("V")
                filler.pump(2)
                for hh in range(2):
                    h = 2 * g + hh
                    for nh in range(2):
                        op("tensor", lambda t, hh=hh, nh=nh, b=b: t.matmul(
                            psO(hh), kvtok[:, hh, 2 + nh, :], qkd[:, hh, nh, :],
                            start=(b == 0 and nh == 0), stop=(nh == 1)),
                            waits=([lastQkd[hh]] if nh == 0 else []))
                    lastIntra[hh] = ptok()
                    lastScanPE = lastIntra[hh]
                    lastOevac[hh] = op("scalar", lambda s_, hh=hh, c0=c0: s_.activation(
                        out_loc[:, hh, c0:c0 + BLK], psO(hh), AF.Copy),
                        waits=[lastIntra[hh], (outStoreTok if b == 0 else None)], inc=("S", 1))
                    if b == 0:
                        op("vector", lambda v, hh=hh: v.tensor_copy(kvf[:, hh, :], psKVd(hh)),
                           waits=[lastIntra[hh], lastOevac[hh], (kvStoreTok[-1] if kvStoreTok else None)],
                           inc=("V", 1))
                    else:
                        op("vector", lambda v, hh=hh, h=h: v.scalar_tensor_tensor(
                            kvf[:, hh, :], kvf[:, hh, :], bd_all[:, h:h + 1], psKVd(hh), ALU.mult, ALU.add),
                            waits=[lastIntra[hh], lastOevac[hh], sConsts], inc=("V", 1))
                    vStt = tok("V")
                    lastCast[hh] = op("scalar", lambda s_, hh=hh: s_.activation(
                        kvbf[:, hh, :], kvf[:, hh, :], AF.Copy), waits=[vStt], inc=("S", 1))
                lastQtilV = vq[1]
                filler.pump(1)

            filler.flush()
            outStoreTok = op("sync", lambda s_, g=g: s_.dma_start(
                out=out_sc[2 * g * 128:(2 * g + 2) * 128, :].rearrange("(hh p) t -> p hh t", p=128),
                in_=out_loc[:, :, :]), waits=[lastOevac[0], lastOevac[1], outStoreTok], inc=("L", 16))
            qtStoreTok = op("sync", lambda s_, g=g: s_.dma_start(
                out=q_sc[2 * g * 128:(2 * g + 2) * 128, :].rearrange("(hh p) t -> p hh t", p=128),
                in_=qt_buf[:, :, :]), waits=[lastQtilV, qtStoreTok], inc=("L", 16))
            kvStoreTok.append(op("sync", lambda s_, g=g: s_.dma_start(
                out=kv_local.ap()[2 * g * 128:(2 * g + 2) * 128, :].rearrange("(hh p) e -> p hh e", p=128),
                in_=kvf[:, :, :]), waits=[lastCast[0], lastCast[1]], inc=("L", 16)))
            projUseTok[g] = lastScanPE

        op("gpsimd", lambda g: g.collective_compute(
            "AllGather", ALU.bypass, replica_groups=[list(range(NCORE))],
            ins=[kv_local.ap().opt()], outs=[kv_all.ap().opt()]),
            waits=[kvStoreTok[-1]], inc=("C", 1))
        ccTok = tok("C")

        p3Done = [x for x in (lastScanPE, lastQtilV, outStoreTok, qtStoreTok, kvStoreTok[-1],
                              lastOevac[0], lastOevac[1], lastQkd[0], lastQkd[1],
                              lastCast[0], lastCast[1], regTok[0], regTok[1],
                              gStore[14], gStore[15], gStop[15]) if x is not None]
        p15Done = p3Done

        h_sb = bview(Bbf, 0, HCN * TFF).rearrange("p (hc t) -> p hc t", t=TFF)
        wgu = {"g": bview(Bbf, 90112, 2048).rearrange("p (dc f) -> p dc f", dc=16),
               "u": bview(Bbf, 94208, 2048).rearrange("p (dc f) -> p dc f", dc=16)}
        wd_b = bview(Bbf, 98304, HCN * 128).rearrange("p (hc f) -> p hc f", hc=HCN)
        gatebf = bview(Bbf, 109568, TFF)
        upbf = bview(Bbf, 111616, TFF)
        kvin32 = bview(Bf, 113664, NH * HD).rearrange("p (h e) -> p h e", h=NH)
        kvinbf = bview(Bbf, 98304, NH * HD).rearrange("p (h e) -> p h e", h=NH)

        mmTok = {}
        evTok = {}
        hTok = {}
        dStopAll = {}
        dEvacAll = {}
        gscStore = {}
        dHalf = {}
        for tt in range(2):
            t0 = tt * TFF
            for hc in range(HCN):
                for mat, srcw, scl in (("g", fg, sg_sb), ("u", fu, su_sb)):
                    w = []
                    if (tt, hc - 1, mat) in mmTok:
                        w.append(mmTok[(tt, hc - 1, mat)])
                    elif tt > 0:
                        w.append(mmTok[(tt - 1, HCN - 1, mat)])
                    else:
                        w += p15Done
                    dmaT = op("gpsimd", lambda gp, hc=hc, srcw=srcw, mat=mat: gp.dma_start(
                        out=wgu[mat][:, :, :], in_=srcw[hc].rearrange("p (dc f) -> p dc f", dc=16)),
                        waits=w, inc=("W", 16))
                    roff = 0 if mat == "g" else 1024
                    w = [dmaT, vXnDone, vRt]
                    if (tt, hc - 1, mat) in evTok:
                        w.append(evTok[(tt, hc - 1, mat)])
                    elif tt > 0:
                        w.append(evTok[(tt - 1, HCN - 1, mat)])
                    else:
                        w += p15Done
                    for dc in range(16):
                        for j in range(2):
                            op("tensor", lambda t, hc=hc, mat=mat, dc=dc, j=j, roff=roff, t0=t0: t.matmul(
                                psA[:, roff + j * 512: roff + (j + 1) * 512],
                                wgu[mat][:, dc, :],
                                xn_ap(dc)[:, t0 + j * 512: t0 + (j + 1) * 512],
                                start=(dc == 0), stop=False),
                                waits=(w if (dc == 0 and j == 0) else []))
                    for j in range(2):
                        op("tensor", lambda t, j=j, roff=roff, t0=t0: t.matmul(
                            psA[:, roff + j * 512: roff + (j + 1) * 512], n128_r[:, :],
                            rt_bf[0:1, t0 + j * 512: t0 + (j + 1) * 512], start=False, stop=True))
                    mmTok[(tt, hc, mat)] = ptok()
                    w = [mmTok[(tt, hc, mat)], lConst]
                    if (tt, hc - 1) in hTok:
                        w.append(hTok[(tt, hc - 1)])
                    elif tt > 0:
                        w.append(hTok[(tt - 1, HCN - 1)])
                        w.append(gscStore[(tt - 1, 14)])
                        w.append(gscStore[(tt - 1, 15)])
                    func = AF.Silu if mat == "g" else AF.Copy
                    dst = gatebf if mat == "g" else upbf
                    evTok[(tt, hc, mat)] = op("scalar", lambda s_, hc=hc, scl=scl, roff=roff,
                                              func=func, dst=dst: s_.activation(
                        dst[:, :], psA[:, roff:roff + TFF], func, scale=scl[:, hc:hc + 1]),
                        waits=w, inc=("S", 1))
                hTok[(tt, hc)] = op("vector", lambda v, hc=hc: v.tensor_mul(
                    h_sb[:, hc, :], gatebf[:, :], upbf[:, :]),
                    waits=[evTok[(tt, hc, "g")], evTok[(tt, hc, "u")]], inc=("V", 1))
                if tt == 1 and exq:
                    exq.pop(0)()
            hAll = hTok[(tt, HCN - 1)]

            for j in range(2):
                for hc in range(HCN):
                    op("tensor", lambda t, j=j, hc=hc: t.matmul(
                        psA[0:1, j * 512:(j + 1) * 512], ones_c[:, :], h_sb[:, hc, j * 512:(j + 1) * 512],
                        start=(hc == 0), stop=(hc == HCN - 1)),
                        waits=([hAll, evTok[(tt, HCN - 1, "g")], evTok[(tt, HCN - 1, "u")]]
                               if (j == 0 and hc == 0) else []))
            pRh = ptok()
            vRh = op("vector", lambda v: v.tensor_copy(rh_bf[:, :], psA[0:1, 0:TFF]), waits=[pRh], inc=("V", 1))

            dStop = {}
            dEvac = {}
            HSPL = 22
            for fc in range(16):
                wA = []
                if (tt, fc - 1) in dHalf:
                    wA.append(dHalf[(tt, fc - 1)])
                elif tt > 0:
                    wA.append(dHalf[(tt - 1, 15)])
                else:
                    wA += p15Done
                fdA = op("gpsimd", lambda gp, fc=fc: gp.dma_start(
                    out=wd_b[:, 0:HSPL, :], in_=fd[fc].rearrange("p (hc f) -> p hc f", hc=HCN)[:, 0:HSPL, :]),
                    waits=wA, inc=("W", 16))
                wB = []
                if fc >= 1:
                    wB.append(dStop[fc - 1])
                elif tt > 0:
                    wB.append(dStopAll[(tt - 1, 15)])
                else:
                    wB += p15Done
                fdB = op("gpsimd", lambda gp, fc=fc: gp.dma_start(
                    out=wd_b[:, HSPL:HCN, :], in_=fd[fc].rearrange("p (hc f) -> p hc f", hc=HCN)[:, HSPL:HCN, :]),
                    waits=wB, inc=("W", 16))
                reg_off = (fc % 2) * 1024
                w = [fdA, hAll, vRh]
                if fc >= 2:
                    w.append(dEvac[fc - 2])
                elif tt > 0:
                    w.append(dEvacAll[(tt - 1, 14 + fc)])
                for hc in range(HCN):
                    for j in range(2):
                        ww = w if (hc == 0 and j == 0) else ([fdB] if (hc == HSPL and j == 0) else [])
                        op("tensor", lambda t, fc=fc, hc=hc, j=j, reg_off=reg_off: t.matmul(
                            psB[:, reg_off + j * 512: reg_off + (j + 1) * 512],
                            wd_b[:, hc, :], h_sb[:, hc, j * 512:(j + 1) * 512],
                            start=(hc == 0), stop=False),
                            waits=ww)
                    if hc == HSPL - 1:
                        dHalf[(tt, fc)] = ptok()
                for j in range(2):
                    op("tensor", lambda t, j=j, reg_off=reg_off: t.matmul(
                        psB[:, reg_off + j * 512: reg_off + (j + 1) * 512], n128_r[:, :],
                        rh_bf[0:1, j * 512:(j + 1) * 512], start=False, stop=True))
                dStop[fc] = ptok()
                dStopAll[(tt, fc)] = dStop[fc]
                w = [dStop[fc]]
                if fc >= 2:
                    w.append(gscStore[(tt, fc - 2)])
                elif tt > 0:
                    w.append(gscStore[(tt - 1, 14 + fc)])
                dbuf = gatebf if fc % 2 == 0 else upbf
                dEvac[fc] = op("scalar", lambda s_, fc=fc, reg_off=reg_off, dbuf=dbuf: s_.activation(
                    dbuf[:, :], psB[:, reg_off:reg_off + TFF], AF.Silu, scale=sd_sb[:, fc:fc + 1]),
                    waits=w, inc=("S", 1))
                dEvacAll[(tt, fc)] = dEvac[fc]
                gscStore[(tt, fc)] = op("sync", lambda s_, fc=fc, t0=t0, dbuf=dbuf: s_.dma_start(
                    out=g_sc[fc * 128:(fc + 1) * 128, t0:t0 + TFF], in_=dbuf[:, :]),
                    waits=[dEvac[fc]], inc=("L", 16))
            if tt == 0:
                for h in range(NH):
                    op("scalar", lambda s_, h=h: s_.activation(
                        scx[:, h, :], mrep[:, :], AF.Exp, scale=neg_s[:, h:h + 1]),
                        waits=([lConst, vSrep] if h == 0 else []), inc=("S", 1))
                sScx = tok("S")
                exq = []
                exstate = {"stt": None}

                def _expkt(cp, hq):
                    w = [ccTok, exstate["stt"]]
                    kvLoad = op("sync", lambda s_, cp=cp, hq=hq: s_.dma_start(
                        out=kvc[:, :, :],
                        in_=kv_all.ap()[cp * 2048 + hq * 512: cp * 2048 + (hq + 1) * 512, :]
                        .rearrange("(h p) e -> p h e", p=128)),
                        waits=w, inc=("L", 16))
                    for h4 in range(4):
                        h = hq * 4 + h4
                        if cp == 0:
                            op("vector", lambda v, h=h, h4=h4, cp=cp: v.tensor_scalar_mul(
                                kvin32[:, h, :], kvc[:, h4, :], scx[:, h, cp:cp + 1]),
                                waits=([kvLoad, sScx] if h4 == 0 else []), inc=("V", 1))
                        else:
                            op("vector", lambda v, h=h, h4=h4, cp=cp: v.scalar_tensor_tensor(
                                kvin32[:, h, :], kvc[:, h4, :], scx[:, h, cp:cp + 1],
                                kvin32[:, h, :], ALU.mult, ALU.add),
                                waits=([kvLoad] if h4 == 0 else []), inc=("V", 1))
                    exstate["stt"] = tok("V")

                for cp in range(NCORE):
                    for hq in range(4):
                        exq.append((lambda cp=cp, hq=hq: _expkt(cp, hq)))
        while exq:
            exq.pop(0)()
        vKvinSum = exstate["stt"]
        vKvin = None
        p2Done = [dStopAll[(1, 15)], dEvacAll[(1, 15)], gscStore[(1, 14)], gscStore[(1, 15)],
                  hTok[(1, HCN - 1)], mmTok[(1, HCN - 1, "g")], mmTok[(1, HCN - 1, "u")],
                  evTok[(1, HCN - 1, "g")], evTok[(1, HCN - 1, "u")]]

        vKvin = op("vector", lambda v: v.tensor_copy(
            kvinbf[:, :, :].rearrange("p a b -> p (a b)"),
            kvin32[:, :, :].rearrange("p a b -> p (a b)")), waits=[vKvinSum] + list(p2Done), inc=("V", 1))
        if debug:
            op("gpsimd", lambda g: g.dma_start(
                out=dbg_kvin, in_=kvin32[:, :, :].rearrange("p a b -> p (a b)")),
                waits=[vKvin], inc=("W", 16))

        olc = [bview(Bbf, 32768, NH * 256).rearrange("p (h t) -> p h t", h=NH),
               bview(Bbf, 40960, NH * 256).rearrange("p (h t) -> p h t", h=NH)]
        qts = [bview(Bbf, 49152, NH * 256).rearrange("p (h t) -> p h t", h=NH),
               bview(Bbf, 57344, NH * 256).rearrange("p (h t) -> p h t", h=NH)]
        gts = [bview(Bbf, 65536, NH * 256).rearrange("p (h t) -> p h t", h=NH),
               bview(Bbf, 73728, NH * 256).rearrange("p (h t) -> p h t", h=NH)]
        oco = [bview(Bbf, 81920, NH * 256).rearrange("p (h t) -> p h t", h=NH),
               bview(Bbf, 90112, NH * 256).rearrange("p (h t) -> p h t", h=NH)]
        row_ss5 = [bview(Bf, 102400, 256)[0:1, :], bview(Bf, 103424, 256)[0:1, :]]
        row_rstd5 = [bview(Bbf, 104448, 256)[0:1, :], bview(Bbf, 104960, 256)[0:1, :]]
        sqt = [bview(Bbf, 105472, 256), bview(Bbf, 105984, 256)]
        utmp = [bview(Bf, 106496, 256), bview(Bf, 107520, 256)]
        nwrow = bview(Bbf, 108544, D)[0:1, :]

        nwTok = op("gpsimd", lambda g: g.dma_start(out=nwrow[:, :], in_=nw), waits=list(p2Done), inc=("W", 16))

        NSL = 8
        sl_u2last = {}
        sl_sqlast = {}
        sl_corrPE = {}
        sl_u1last = {}
        sl_sss = {}
        sl_vrec = {}
        corrV = {}
        prevOuterPE = {}
        sl_u2last_pe = None
        icorr_of = {}

        def p5_load_corr(sl):
            par = sl % 2
            qc = sl * 256
            ldTok = []
            for dram, dst in ((out_sc, olc), (q_sc, qts), (gate_sc, gts)):
                for i in range(2):
                    w = []
                    if i == 0 and dram is out_sc:
                        if sl == 0:
                            w = list(p2Done) + [vKvin]
                        elif sl == 1:
                            w = [vKvin]
                        else:
                            w = [sl_u1last[sl - 2], sl_sqlast[sl - 2], sl_corrPE[sl - 2]]
                    ldTok.append(op("sync", lambda s_, dram=dram, dst=dst, qc=qc, i=i, par=par: s_.dma_start(
                        out=dst[par][:, 8 * i:8 * i + 8, :],
                        in_=dram[:, qc:qc + 256].rearrange("(h p) t -> p h t", p=128)[:, 8 * i:8 * i + 8, :]),
                        waits=w, inc=("L", 16)))
            icorr0 = sl * NH
            icorr_of[sl] = icorr0
            for h in range(NH):
                ic = icorr0 + h
                roff = (ic % 4) * 512
                w = [vKvin, ldTok, corrV.get(ic - 4)]
                if sl == 0 and h < 4:
                    w += list(p2Done)
                op("tensor", lambda t, h=h, roff=roff, par=par: t.matmul(
                    psA[:, roff:roff + 256], kvinbf[:, h, :], qts[par][:, h, :],
                    start=True, stop=True), waits=w)
                pC = ptok()
                if h == NH - 1:
                    sl_corrPE[sl] = pC
                w = [pC, sConsts]
                if h == 0 and sl >= 2:
                    w.append(sl_sqlast[sl - 2])
                corrV[ic] = op("vector", lambda v, h=h, sl=sl, par=par, roff=roff: v.scalar_tensor_tensor(
                    oco[par][:, h, :], psA[:, roff:roff + 256], bdg[:, sl, h:h + 1],
                    olc[par][:, h, :], ALU.mult, ALU.add), waits=w, inc=("V", 1))

        def p5_rms_u(sl):
            par = sl % 2
            qc = sl * 256
            icorr0 = icorr_of[sl]
            ssreg = psB[0:1, 0:256] if par == 0 else psB[0:1, 1536:1792]
            for h in range(NH):
                if h >= 2:
                    prd = prevOuterPE.get((sl, "sq", h - 2))
                else:
                    prd = prevOuterPE.get((sl - 1, "sq", NH - 2 + h))
                w = [corrV[icorr0 + h], prd]
                w = [x for x in w if x is not None]
                sq = op("scalar", lambda s_, h=h, par=par: s_.activation(
                    sqt[h % 2][:, :], oco[par][:, h, :], AF.Square), waits=w, inc=("S", 1))
                if h == NH - 1:
                    sl_sqlast[sl] = sq
                ww = [sq]
                if h == 0 and sl >= 2:
                    ww.append(sl_sss[sl - 2])
                op("tensor", lambda t, h=h, ssreg=ssreg: t.matmul(
                    ssreg, ones_c[:, :], sqt[h % 2][:, :], start=(h == 0), stop=(h == NH - 1)),
                    waits=ww)
                prevOuterPE[(sl, "sq", h)] = ptok()
            sl_sss[sl] = op("scalar", lambda s_, ssreg=ssreg, par=par: s_.activation(
                row_ss5[par][:, :], ssreg, AF.Sqrt, scale=1.0 / D, bias=eps_rms[:, :]),
                waits=[prevOuterPE[(sl, "sq", NH - 1)]], inc=("S", 1))
            w = [sl_sss[sl]]
            if sl >= 2:
                w.append(sl_u2last_pe)
            op("vector", lambda v, par=par: v.reciprocal(utmp[0][0:1, :], row_ss5[par][:, :]),
               waits=w, inc=("V", 1))
            sl_vrec[sl] = op("vector", lambda v, par=par: v.tensor_copy(row_rstd5[par][:, :], utmp[0][0:1, :]),
                             inc=("V", 1))
            u2T = {}
            pOut = None
            u1 = None
            for oc in range(16):
                roff = 512 + (oc % 2) * 512
                w = [sl_vrec[sl], nwTok, u2T.get(oc - 2)]
                op("tensor", lambda t, oc=oc, roff=roff, par=par: t.matmul(
                    psB[:, roff:roff + 256], nwrow[0:1, oc * 128:(oc + 1) * 128], row_rstd5[par][:, :],
                    start=True, stop=True), waits=w)
                pOut = ptok()
                u1 = op("vector", lambda v, oc=oc, par=par: v.tensor_mul(
                    utmp[oc % 2][:, :], oco[par][:, oc, :], gts[par][:, oc, :]),
                    waits=[corrV[icorr0 + oc], sl_corrPE[sl]], inc=("V", 1))
                u2T[oc] = op("vector", lambda v, oc=oc, qc=qc, roff=roff: v.tensor_mul(
                    u_ap(oc)[:, qc:qc + 256], utmp[oc % 2][:, :], psB[:, roff:roff + 256]),
                    waits=[pOut], inc=("V", 1))
            sl_u1last[sl] = u1
            sl_u2last[sl] = u2T[15]
            return pOut

        for sl in range(NSL):
            p5_load_corr(sl)
            if sl >= 1:
                sl_u2last_pe = p5_rms_u(sl - 1)
        sl_u2last_pe = p5_rms_u(NSL - 1)
        uDone = sl_u2last[NSL - 1]

        wo_b = [bview(Bbf, 32768, 2048).rearrange("p (oc f) -> p oc f", oc=16),
                bview(Bbf, 36864, 2048).rearrange("p (oc f) -> p oc f", oc=16)]
        gbf6 = [bview(Bbf, 40960, T), bview(Bbf, 45056, T)]
        xc6 = [bview(Bf, 49152, T), bview(Bf, 57344, T)]
        osb = [bview(Bf, 65536, T), bview(Bf, 73728, T)]
        tmp6 = [bview(Bf, 81920, T), bview(Bf, 90112, T)]

        mmStop = {}
        p1Tok = {}
        p2Tok = {}
        stTok = {}
        for fc in range(16):
            w = [uDone] if fc < 2 else [mmStop[fc - 2]]
            woT = op("gpsimd", lambda gp, fc=fc: gp.dma_start(
                out=wo_b[fc % 2][:, :, :], in_=wo[fc].rearrange("p (oc f) -> p oc f", oc=16)),
                waits=w, inc=("W", 16))
            w = [uDone] if fc < 2 else [p1Tok[fc - 2]]
            gl = op("sync", lambda s_, fc=fc: s_.dma_start(
                out=gbf6[fc % 2][:, :], in_=g_sc[fc * 128:(fc + 1) * 128, :]), waits=w, inc=("L", 16))
            xl = op("sync", lambda s_, fc=fc: s_.dma_start(
                out=xc6[fc % 2][:, :], in_=xt[fc * 128:(fc + 1) * 128, :]),
                waits=([uDone] if fc < 2 else [p2Tok[fc - 2]]), inc=("L", 16))
            reg = psA if fc % 2 == 0 else psB
            w = [woT, uDone]
            if fc >= 2:
                w.append(p1Tok[fc - 2])
            for oc in range(16):
                for j in range(4):
                    op("tensor", lambda t, fc=fc, oc=oc, j=j, reg=reg: t.matmul(
                        reg[:, j * 512:(j + 1) * 512], wo_b[fc % 2][:, oc, :],
                        u_ap(oc)[:, j * 512:(j + 1) * 512], start=(oc == 0), stop=(oc == 15)),
                        waits=(w if (oc == 0 and j == 0) else [])),
            mmStop[fc] = ptok()
            p1Tok[fc] = op("vector", lambda v, fc=fc, reg=reg: v.tensor_mul(
                tmp6[fc % 2][:, :], reg[:, :], gbf6[fc % 2][:, :]), waits=[mmStop[fc], gl], inc=("V", 1))
            p2Tok[fc] = op("vector", lambda v, fc=fc: v.tensor_add(
                osb[fc % 2][:, :], tmp6[fc % 2][:, :], xc6[fc % 2][:, :]),
                waits=[xl, stTok.get(fc - 2)], inc=("V", 1))
            stTok[fc] = op("sync", lambda s_, fc=fc: s_.dma_start(
                out=out_d[fc * 128:(fc + 1) * 128, :], in_=osb[fc % 2][:, :]), waits=[p2Tok[fc]], inc=("O", 16))

        for (s, v) in qtoks("O"):
            op("sync", lambda s_, s0=s, v0=v: s_.wait_ge(sem[s0], v0))

        block = e(nc.Block())

        SELF_SER = {"vector": "V", "scalar": "S"}

        def run_engine(eng_obj, name):
            own = SELF_SER.get(name)
            nown = 0
            for (engname, ws, fn, inc) in ops:
                if engname != name:
                    continue
                if own is not None and nown > 0:
                    eng_obj.wait_ge(sem[own], nown)
                for (s, v) in ws:
                    eng_obj.wait_ge(sem[s], v)
                ins = fn(eng_obj)
                if inc is not None and ins is not None:
                    ins.then_inc(sem[inc[0]], inc[1])
                    if own is not None and inc[0] == own:
                        nown += inc[1]

        block.sync(lambda s_: run_engine(s_, "sync"))
        block.gpsimd(lambda g: run_engine(g, "gpsimd"))
        block.tensor(lambda t: run_engine(t, "tensor"))
        block.vector(lambda v: run_engine(v, "vector"))
        block.scalar(lambda s_: run_engine(s_, "scalar"))

    return nc


def _prep_inputs(inputs):
    x = np.asarray(inputs["L0_attn_hidden_states"])
    wqkv = np.asarray(inputs["L0_attn_w_qkv"])
    wgate = np.asarray(inputs["L0_attn_w_gate"])
    wout = np.asarray(inputs["L0_attn_w_out"])
    normw = np.asarray(inputs["L0_attn_norm_w"])
    slope = np.asarray(inputs["L0_attn_slope_rate"])
    fgq = np.asarray(inputs["L0_ffn_wq_gate"])
    fuq = np.asarray(inputs["L0_ffn_wq_up"])
    fdq = np.asarray(inputs["L0_ffn_wq_down"])
    sgv = np.asarray(inputs["L0_ffn_s_gate"])
    suv = np.asarray(inputs["L0_ffn_s_up"])
    sdv = np.asarray(inputs["L0_ffn_s_down"])

    xT = np.ascontiguousarray(x[0].T)

    wq3 = wqkv.reshape(16, 3, 128, 2048)
    blocks = []
    for g in range(8):
        rows = np.concatenate([wq3[2 * g, 0], wq3[2 * g + 1, 0],
                               wq3[2 * g, 1], wq3[2 * g + 1, 1],
                               wq3[2 * g, 2], wq3[2 * g + 1, 2]], axis=0)
        blocks.append(rows.reshape(6, 128, 16, 128).transpose(3, 0, 2, 1))
    wqkv_b = np.ascontiguousarray(np.stack(blocks)).reshape(8, 128, 6 * 16 * 128)

    wg_b = np.ascontiguousarray(
        wgate.reshape(16, 128, 16, 128).transpose(0, 3, 2, 1)).reshape(16, 128, 16 * 128)
    wo_b = np.ascontiguousarray(
        wout.reshape(16, 128, 16, 128).transpose(0, 3, 2, 1)).reshape(16, 128, 16 * 128)
    fg_b = np.ascontiguousarray(
        fgq.reshape(44, 128, 16, 128).transpose(0, 3, 2, 1)).reshape(44, 128, 16 * 128)
    fu_b = np.ascontiguousarray(
        fuq.reshape(44, 128, 16, 128).transpose(0, 3, 2, 1)).reshape(44, 128, 16 * 128)
    fd_b = np.ascontiguousarray(
        fdq.reshape(16, 128, 44, 128).transpose(0, 3, 2, 1)).reshape(16, 128, 44 * 128)

    sg_l = np.ascontiguousarray(sgv.reshape(44, 128).T)
    su_l = np.ascontiguousarray(suv.reshape(44, 128).T)
    sd_l = np.ascontiguousarray(sdv.reshape(16, 128).T)
    nw_l = np.ascontiguousarray(normw.reshape(1, 2048))
    sr_l = np.ascontiguousarray(slope.reshape(1, 16))

    in_maps = []
    for c in range(NCORE):
        m = np.full((1, NCORE), 1e9, dtype=np.float32)
        for cp in range(c):
            m[0, cp] = 2048.0 * (c - 1 - cp)
        in_maps.append(dict(
            xt=np.ascontiguousarray(xT[:, c * T:(c + 1) * T]),
            wqkv=wqkv_b, wg=wg_b, wo=wo_b, fg=fg_b, fu=fu_b, fd=fd_b,
            sg=sg_l, su=su_l, sd=sd_l, nw=nw_l, sr=sr_l, mrow=m))
    return in_maps


def kernel(**inputs):
    in_maps = _prep_inputs(inputs)
    nc = build(debug=False)
    res = run_bass_kernel_spmd(nc, in_maps, core_ids=list(range(NCORE)))
    outT = np.concatenate([r["out"] for r in res.results], axis=1)
    return np.ascontiguousarray(outT.T)[None].astype(np.float32)


# version 16
# speedup vs baseline: 1.0150x; 1.0150x over previous
import numpy as np
import ml_dtypes
from contextlib import ExitStack

import concourse.bass as bass
import concourse.mybir as mybir
from concourse.bass_utils import run_bass_kernel_spmd

F32 = mybir.dt.float32
BF16 = mybir.dt.bfloat16
I32 = mybir.dt.int32
U8 = mybir.dt.uint8
AF = mybir.ActivationFunctionType
ALU = mybir.AluOpType

NCORE = 8
D = 2048
T = 2048
NH = 16
HD = 128
BLK = 256
NB = T // BLK
NG = 8
HID = 5632
HCN = 44
TFF = 1024
LN_EPS = 1e-5
RMS_EPS = 1e-6


def _bf16_bits(x):
    return np.ascontiguousarray(x.astype(ml_dtypes.bfloat16).view(np.uint16))


def build(debug=False):
    nc = bass.Bass("TRN2", target_bir_lowering=False)

    xt = nc.dram_tensor("xt", [D, T], F32, kind="ExternalInput").ap()
    wqkv = nc.dram_tensor("wqkv", [NG, 128, 6 * 16 * 128], F32, kind="ExternalInput").ap()
    wg = nc.dram_tensor("wg", [16, 128, 16 * 128], F32, kind="ExternalInput").ap()
    wo = nc.dram_tensor("wo", [16, 128, 16 * 128], F32, kind="ExternalInput").ap()
    fg = nc.dram_tensor("fg", [HCN, 128, 16 * 128], I32, kind="ExternalInput").ap()
    fu = nc.dram_tensor("fu", [HCN, 128, 16 * 128], I32, kind="ExternalInput").ap()
    fd = nc.dram_tensor("fd", [16, 128, HCN * 128], I32, kind="ExternalInput").ap()
    sg = nc.dram_tensor("sg", [128, HCN], F32, kind="ExternalInput").ap()
    su = nc.dram_tensor("su", [128, HCN], F32, kind="ExternalInput").ap()
    sd = nc.dram_tensor("sd", [128, 16], F32, kind="ExternalInput").ap()
    nw = nc.dram_tensor("nw", [1, D], F32, kind="ExternalInput").ap()
    sr = nc.dram_tensor("sr", [1, NH], F32, kind="ExternalInput").ap()
    mrow = nc.dram_tensor("mrow", [1, NCORE], F32, kind="ExternalInput").ap()

    out_d = nc.dram_tensor("out", [D, T], F32, kind="ExternalOutput").ap()

    ikind = "ExternalOutput" if debug else "Internal"
    if debug == "kinds":
        debug = False
        ikind = "ExternalOutput"
    gate_sc = nc.dram_tensor("gate_sc", [D, T], BF16, kind=ikind).ap()
    out_sc = nc.dram_tensor("out_sc", [D, T], BF16, kind=ikind).ap()
    q_sc = nc.dram_tensor("q_sc", [D, T], BF16, kind=ikind).ap()
    g_sc = nc.dram_tensor("g_sc", [D, T], BF16, kind=ikind).ap()
    kv_local = nc.dram_tensor("kv_local", [NH * HD, HD], F32, kind="Internal")
    kv_all = nc.dram_tensor("kv_all", [NCORE * NH * HD, HD], F32, kind="Internal", addr_space="Shared")
    if debug:
        dbg_xn = nc.dram_tensor("dbg_xn", [128, 16 * T], F32, kind="ExternalOutput").ap()
        dbg_rt = nc.dram_tensor("dbg_rt", [1, T], F32, kind="ExternalOutput").ap()
        dbg_kvin = nc.dram_tensor("dbg_kvin", [128, NH * HD], F32, kind="ExternalOutput").ap()

    iden_c = nc.inline_tensor(_bf16_bits(np.eye(128, dtype=np.float32)), name="idenb")
    arr_c = nc.inline_tensor(_bf16_bits(
        np.broadcast_to(np.arange(1, BLK + 1, dtype=np.float32), (128, BLK)).copy()), name="arrrep")
    n_idx = (np.arange(128)[:, None, None] + 128 * np.arange(2)[None, :, None]).astype(np.float32)
    m_idx = np.broadcast_to(np.arange(BLK, dtype=np.float32), (128, 2, BLK))
    idxT_np = np.where(m_idx >= n_idx, m_idx - n_idx, np.float32(1e9)).astype(np.float32)
    idxT_c = nc.inline_tensor(_bf16_bits(np.ascontiguousarray(idxT_np)), name="idxT")
    karr_np = np.broadcast_to(
        (255.0 - (np.arange(128)[:, None, None] + 128 * np.arange(2)[None, :, None])), (128, 2, 128)
    ).astype(np.float32)
    karr_c = nc.inline_tensor(_bf16_bits(np.ascontiguousarray(karr_np)), name="karr")

    with ExitStack() as stk:
        e = stk.enter_context
        iden = e(nc.sbuf_tensor("iden", [128, 128], BF16))
        ones_c = e(nc.sbuf_tensor("ones_c", [128, 1], BF16))
        ones_rf = e(nc.sbuf_tensor("ones_rf", [1, 128], F32))
        n128_r = e(nc.sbuf_tensor("n128_r", [1, 128], BF16))
        srow = e(nc.sbuf_tensor("srow", [1, NH], F32))
        s_rep = e(nc.sbuf_tensor("s_rep", [128, NH], F32))
        neg_s = e(nc.sbuf_tensor("neg_s", [128, NH], F32))
        bd_all = e(nc.sbuf_tensor("bd_all", [128, NH], F32))
        bdg = e(nc.sbuf_tensor("bdg", [128, NB, NH], F32))
        mrep = e(nc.sbuf_tensor("mrep", [128, NCORE], F32))
        scx = e(nc.sbuf_tensor("scx", [128, NH, NCORE], F32))
        sg_sb = e(nc.sbuf_tensor("sg_sb", [128, HCN], F32))
        su_sb = e(nc.sbuf_tensor("su_sb", [128, HCN], F32))
        sd_sb = e(nc.sbuf_tensor("sd_sb", [128, 16], F32))
        arr_sb = e(nc.sbuf_tensor("arr_sb", [128, BLK], BF16))
        idxT_sb = e(nc.sbuf_tensor("idxT_sb", [128, 2, BLK], BF16))
        karr_sb = e(nc.sbuf_tensor("karr_sb", [128, 2, 128], BF16))
        rt_bf = e(nc.sbuf_tensor("rt_bf", [1, T], BF16))
        rh_bf = e(nc.sbuf_tensor("rh_bf", [1, TFF], BF16))
        qdec = e(nc.sbuf_tensor("qdec", [128, 2, BLK], BF16))
        diagT = e(nc.sbuf_tensor("diagT", [128, 2, 2, BLK], BF16))
        kdecrep = e(nc.sbuf_tensor("kdecrep", [128, 2, 2, 128], BF16))
        kvf = e(nc.sbuf_tensor("kvf", [128, 2, HD], F32))
        kvbf = e(nc.sbuf_tensor("kvbf", [128, 2, HD], BF16))
        kvtok = e(nc.sbuf_tensor("kvtok", [128, 2, 4, 128], BF16))
        ktl = e(nc.sbuf_tensor("ktl", [128, 2, 2, 128], BF16))
        qkd = e(nc.sbuf_tensor("qkd", [128, 2, 2, BLK], BF16))
        eps_ln = e(nc.sbuf_tensor("eps_ln", [1, 1], F32))
        kvc = e(nc.sbuf_tensor("kvc", [128, 4, HD], F32))
        eps_rms = e(nc.sbuf_tensor("eps_rms", [1, 1], F32))

        Abuf = e(nc.sbuf_tensor("Abuf", [128, 65536], U8))
        Bbuf = e(nc.sbuf_tensor("Bbuf", [128, 121856], U8))
        Abf = Abuf.bitcast(BF16).ap()

        def xn_ap(dc):
            return Abf[:, dc * T:(dc + 1) * T]
        u_ap = xn_ap

        Bbf = Bbuf.bitcast(BF16).ap()
        Bf = Bbuf.bitcast(F32).ap()

        def bview(dt_ap, byte_off, n):
            esz = 2 if dt_ap is Bbf else 4
            assert byte_off % esz == 0
            return dt_ap[:, byte_off // esz: byte_off // esz + n]

        psA = e(nc.psum_tensor("psA", [128, 2048], F32))
        psB = e(nc.psum_tensor("psB", [128, 2048], F32))

        POOL = {"L": 6, "W": 6, "O": 2}
        sem = {}
        for s in ["P", "V", "S", "C"]:
            sem[s] = e(nc.semaphore(s))
        for q, n in POOL.items():
            for i in range(n):
                sem[f"{q}{i}"] = e(nc.semaphore(f"{q}{i}"))
        cnt = {s: 0 for s in sem}
        pool_idx = {q: 0 for q in POOL}
        OWN = {"P": "tensor", "V": "vector", "S": "scalar"}
        ops = []

        def op(eng, fn, waits=(), inc=None):
            flat = []
            for wt in waits:
                if wt is None:
                    continue
                if isinstance(wt, list):
                    flat.extend(x for x in wt if x is not None)
                else:
                    flat.append(wt)
            if inc is not None and inc[0] in POOL:
                q = inc[0]
                i = pool_idx[q]
                pool_idx[q] = (i + 1) % POOL[q]
                name = f"{q}{i}"
                if cnt[name] > 0:
                    flat.append((name, cnt[name]))
                inc = (name, 16)
            ws = []
            for (s, v) in flat:
                if v <= 0 or (s in OWN and OWN[s] == eng):
                    continue
                assert v <= cnt[s], f"wait on future token {s}={v} (now {cnt[s]})"
                if (s, v) not in ws:
                    ws.append((s, v))
            ops.append((eng, ws, fn, inc))
            if inc is not None:
                cnt[inc[0]] += inc[1]
                return (inc[0], cnt[inc[0]])
            return None

        def tok(s):
            return (s, cnt[s])

        def qtoks(q):
            return [(f"{q}{i}", cnt[f"{q}{i}"]) for i in range(POOL[q]) if cnt[f"{q}{i}"] > 0]

        _last_ptok = [None]

        def ptok():
            for i in range(len(ops) - 1, -1, -1):
                if ops[i][0] == "tensor":
                    if ops[i][3] is not None:
                        return _last_ptok[0]
                    ops[i] = (ops[i][0], ops[i][1], ops[i][2], ("P", 1))
                    break
            else:
                raise AssertionError("no tensor op to attach token to")
            cnt["P"] += 1
            _last_ptok[0] = ("P", cnt["P"])
            return _last_ptok[0]

        op("sync", lambda s_: s_.dma_start(out=iden[:, :], in_=iden_c.ap().bitcast(BF16)), inc=("L", 16))
        op("sync", lambda s_: s_.dma_start(out=arr_sb[:, :], in_=arr_c.ap().bitcast(BF16)), inc=("L", 16))
        op("sync", lambda s_: s_.dma_start(out=idxT_sb[:, :, :], in_=idxT_c.ap().bitcast(BF16)), inc=("L", 16))
        op("sync", lambda s_: s_.dma_start(out=karr_sb[:, :, :], in_=karr_c.ap().bitcast(BF16)), inc=("L", 16))
        op("sync", lambda s_: s_.dma_start(out=srow[:, :], in_=sr), inc=("L", 16))
        op("sync", lambda s_: s_.dma_start(out=sg_sb[:, :], in_=sg), inc=("L", 16))
        op("sync", lambda s_: s_.dma_start(out=su_sb[:, :], in_=su), inc=("L", 16))
        op("sync", lambda s_: s_.dma_start(out=sd_sb[:, :], in_=sd), inc=("L", 16))
        op("sync", lambda s_: s_.dma_start(out=mrep[:, :], in_=mrow.broadcast_to([128, NCORE])), inc=("L", 16))
        lConst = qtoks("L")

        op("vector", lambda v: v.memset(ones_c[:, :], 1.0), inc=("V", 1))
        op("vector", lambda v: v.memset(ones_rf[:, :], 1.0), inc=("V", 1))
        op("vector", lambda v: v.memset(n128_r[:, :], -128.0), inc=("V", 1))
        op("vector", lambda v: v.memset(eps_ln[:, :], LN_EPS), inc=("V", 1))
        op("vector", lambda v: v.memset(eps_rms[:, :], RMS_EPS), inc=("V", 1))
        vMemset = tok("V")

        op("tensor", lambda t: t.matmul(psB[:, 0:NH], ones_rf[:, :], srow[:, :], start=True, stop=True),
           waits=[lConst, vMemset])
        pSrep = ptok()
        op("vector", lambda v: v.tensor_copy(s_rep[:, :], psB[:, 0:NH]), waits=[pSrep], inc=("V", 1))
        op("vector", lambda v: v.tensor_scalar_mul(neg_s[:, :], s_rep[:, :], -1.0), inc=("V", 1))
        vSrep = tok("V")
        op("scalar", lambda s_: s_.activation(bd_all[:, :], s_rep[:, :], AF.Exp, scale=-256.0),
           waits=[vSrep], inc=("S", 1))
        for b in range(NB):
            op("scalar", lambda s_, b=b: s_.activation(bdg[:, b, :], s_rep[:, :], AF.Exp, scale=-256.0 * b),
               inc=("S", 1))
        sConsts = tok("S")

        XBF = bview(Bbf, 0, 16 * T).rearrange("p (dc t) -> p dc t", t=T)
        SQ = [bview(Bbf, 65536, T), bview(Bbf, 69632, T)]
        row_mean = bview(Bf, 73728, T)[0:1, :]
        row_b = bview(Bf, 81920, T)[0:1, :]
        tmpP1 = bview(Bf, 90112, T)
        std_row = tmpP1[0:1, :]
        rstd_rep = bview(Bf, 98304, T)
        nm_rep = bview(Bf, 106496, T)

        xbf_dma = []
        for i in range(8):
            xbf_dma.append(op("gpsimd", lambda g, i=i: g.dma_start(
                out=XBF[:, 2 * i:2 * i + 2, :],
                in_=xt.rearrange("(dc p) t -> p dc t", p=128)[:, 2 * i:2 * i + 2, :]), inc=("W", 16)))

        statsTok = []
        for dc in range(16):
            wq_ = [xbf_dma[dc // 2]]
            if dc >= 2:
                wq_.append(statsTok[dc - 2])
            sqT = op("vector", lambda v, dc=dc: v.tensor_mul(SQ[dc % 2][:, :], XBF[:, dc, :], XBF[:, dc, :]),
                     waits=wq_, inc=("V", 1))
            w = [xbf_dma[dc // 2], vMemset, vSrep]
            for j in range(4):
                op("tensor", lambda t, dc=dc, j=j: t.matmul(
                    psB[0:1, j * 512:(j + 1) * 512], ones_c[:, :], XBF[:, dc, j * 512:(j + 1) * 512],
                    start=(dc == 0), stop=(dc == 15)), waits=(w if j == 0 else []))
            for j in range(4):
                op("tensor", lambda t, dc=dc, j=j: t.matmul(
                    psA[0:1, j * 512:(j + 1) * 512], ones_c[:, :], SQ[dc % 2][:, j * 512:(j + 1) * 512],
                    start=(dc == 0), stop=(dc == 15)), waits=([sqT] if j == 0 else []))
            statsTok.append(ptok())
        pStats = statsTok[-1]

        op("scalar", lambda s_: s_.activation(row_mean[:, :], psB[0:1, :], AF.Copy, scale=1.0 / D),
           waits=[pStats], inc=("S", 1))
        op("scalar", lambda s_: s_.activation(row_b[:, :], psB[0:1, :], AF.Square, scale=1.0 / D),
           inc=("S", 1))
        sRows1 = tok("S")
        op("vector", lambda v: v.scalar_tensor_tensor(row_b[:, :], psA[0:1, :], 1.0 / D, row_b[:, :],
                                                      ALU.mult, ALU.subtract), waits=[sRows1], inc=("V", 1))
        vVar = tok("V")
        op("scalar", lambda s_: s_.activation(std_row[:, :], row_b[:, :], AF.Sqrt, bias=eps_ln[:, :]),
           waits=[vVar], inc=("S", 1))
        sStd = tok("S")
        op("vector", lambda v: v.reciprocal(row_b[:, :], std_row[:, :]), waits=[sStd], inc=("V", 1))
        op("vector", lambda v: v.scalar_tensor_tensor(row_mean[:, :], row_mean[:, :], -1.0, row_b[:, :],
                                                      ALU.mult, ALU.mult), inc=("V", 1))
        vRowsDone = tok("V")

        for j in range(4):
            op("tensor", lambda t, j=j: t.matmul(psA[:, j * 512:(j + 1) * 512], ones_rf[:, :],
                                                 row_b[0:1, j * 512:(j + 1) * 512], start=True, stop=True),
               waits=([vRowsDone] if j == 0 else []))
        for j in range(4):
            op("tensor", lambda t, j=j: t.matmul(psB[:, j * 512:(j + 1) * 512], ones_rf[:, :],
                                                 row_mean[0:1, j * 512:(j + 1) * 512], start=True, stop=True))
        pBc = ptok()
        op("vector", lambda v: v.tensor_copy(rstd_rep[:, :], psA[:, :]), waits=[pBc], inc=("V", 1))
        vRstdRep = op("vector", lambda v: v.tensor_copy(nm_rep[:, :], psB[:, :]), inc=("V", 1))
        vNmRep = vRstdRep

        DCO = [6, 7, 8, 9, 10, 11, 12, 13, 14, 15, 0, 1, 2, 3, 4, 5]
        applyTok = {}
        for k, dc in enumerate(DCO):
            op("vector", lambda v, dc=dc: v.tensor_mul(tmpP1[:, :], XBF[:, dc, :], rstd_rep[:, :]),
               waits=([pBc] if k == 0 else []), inc=("V", 1))
            applyTok[dc] = op("vector", lambda v, dc=dc: v.tensor_add(
                xn_ap(dc)[:, :], tmpP1[:, :], nm_rep[:, :]), inc=("V", 1))
        vXnDone = tok("V")

        if debug:
            op("gpsimd", lambda g: g.dma_start(out=dbg_xn, in_=Abf[:, :]), waits=[vXnDone], inc=("W", 16))

        qkv_sb = bview(Bbf, 0, 6 * T).rearrange("p (fc t) -> p fc t", t=T)
        qkvw = [bview(Bbf, 24576, 12288).rearrange("p (fc dc f) -> p fc dc f", fc=6, dc=16),
                bview(Bbf, 49152, 12288).rearrange("p (fc dc f) -> p fc dc f", fc=6, dc=16)]
        out_loc = bview(Bbf, 73728, 2 * T).rearrange("p (hh t) -> p hh t", t=T)
        qt_buf = bview(Bbf, 81920, 2 * T).rearrange("p (hh t) -> p hh t", t=T)

        def psO(hh):
            return psB[:, hh * 1024: hh * 1024 + 256]

        def psKVd(hh):
            return psB[:, hh * 1024 + 256: hh * 1024 + 384]

        def psQK(hh):
            return psB[:, hh * 1024 + 512: hh * 1024 + 1024]

        def psQKbf(hh):
            return psB.ap()[:, hh * 1024 + 512: hh * 1024 + 1024].bitcast(BF16)

        wgbuf = [bview(Bbf, 90112, 2048).rearrange("p (dc f) -> p dc f", dc=16),
                 bview(Bbf, 94208, 2048).rearrange("p (dc f) -> p dc f", dc=16)]
        gatebuf = [bview(Bbf, 98304, T), bview(Bbf, 102400, T)]
        wgDmaT = {}
        gStop = {}
        gStore = {}
        gEvacH2 = {}

        projUseTok = {}
        regTok = {0: None, 1: None}
        lastScanPE = None
        lastQtilV = None
        lastOevac = {0: None, 1: None}
        lastQkd = {0: None, 1: None}
        lastIntra = {0: None, 1: None}
        lastCast = {0: None, 1: None}
        kvStoreTok = []
        outStoreTok = None
        qtStoreTok = None

        vRt = None
        for g in range(NG):
            if g == 1:
                w0 = [vXnDone, lastOevac[0], lastOevac[1], lastQkd[0], lastQkd[1],
                      lastCast[0], lastCast[1]]
                for j in range(4):
                    for dc in range(16):
                        op("tensor", lambda t, j=j, dc=dc: t.matmul(
                            psB[0:1, j * 512:(j + 1) * 512], ones_c[:, :],
                            xn_ap(dc)[:, j * 512:(j + 1) * 512],
                            start=(dc == 0), stop=(dc == 15)),
                            waits=(w0 if (j == 0 and dc == 0) else []))
                pRt = ptok()
                vRt = op("vector", lambda v: v.tensor_copy(rt_bf[:, :], psB[0:1, :]), waits=[pRt], inc=("V", 1))
                if debug:
                    op("gpsimd", lambda g_: g_.dma_start(out=dbg_rt, in_=rt_bf[:, :]), waits=[vRt], inc=("W", 16))
            par = g % 2
            wDma = []
            for i in range(3):
                if g >= 2:
                    w = [projUseTok.get(g - 2)] if i == 0 else []
                elif g == 0:
                    w = [applyTok[min(15, 7 + 2 * i)]]
                else:
                    w = [applyTok[min(15, 13 + 2 * i)]] if i < 2 else [applyTok[5], statsTok[15]]
                wDma.append(op("gpsimd", lambda gp, g=g, i=i, par=par: gp.dma_start(
                    out=qkvw[par][:, 2 * i:2 * i + 2, :, :],
                    in_=wqkv[g].rearrange("p (fc dc f) -> p fc dc f", fc=6, dc=16)[:, 2 * i:2 * i + 2, :, :]),
                    waits=w, inc=("W", 16)))

            decTok = {}
            for hh in range(2):
                h = 2 * g + hh
                op("scalar", lambda s_, hh=hh, h=h: s_.activation(
                    qdec[:, hh, :], arr_sb[:, :], AF.Exp, scale=neg_s[:, h:h + 1]),
                    waits=([lConst, vSrep, sConsts] if hh == 0 else []), inc=("S", 1))
                op("scalar", lambda s_, hh=hh, h=h: s_.activation(
                    diagT[:, hh, :, :], idxT_sb[:, :, :], AF.Exp, scale=neg_s[:, h:h + 1]), inc=("S", 1))
                decTok[hh] = op("scalar", lambda s_, hh=hh, h=h: s_.activation(
                    kdecrep[:, hh, :, :], karr_sb[:, :, :], AF.Exp, scale=neg_s[:, h:h + 1]), inc=("S", 1))

            evacTok = {}
            for fc in range(6):
                for half in range(2):
                    r = (fc * 2 + half) % 2
                    w = [wDma[fc // 2], regTok[r]]
                    dcs = DCO if (g == 0 and fc == 0 and half == 0) else list(range(16))
                    for kdc, dc in enumerate(dcs):
                        for j in range(2):
                            ww = w if (kdc == 0 and j == 0) else []
                            if g == 0 and fc == 0 and half == 0 and j == 0:
                                ww = ww + [applyTok[dc]]
                                if kdc == 0:
                                    ww = ww + [vRstdRep]
                            op("tensor", lambda t, par=par, fc=fc, half=half, dc=dc, j=j, r=r,
                               kdc=kdc: t.matmul(
                                psA[:, r * 1024 + j * 512: r * 1024 + (j + 1) * 512],
                                qkvw[par][:, fc, dc, :],
                                xn_ap(dc)[:, half * 1024 + j * 512: half * 1024 + (j + 1) * 512],
                                start=(kdc == 0), stop=(kdc == 15)),
                                waits=ww)
                    pStop = ptok()
                    we = [pStop]
                    if g > 0 and fc == 0 and half == 0:
                        we += [lastScanPE, lastQtilV]
                    evacTok[(fc, half)] = op("scalar", lambda s_, fc=fc, half=half, r=r: s_.activation(
                        qkv_sb[:, fc, half * 1024:(half + 1) * 1024], psA[:, r * 1024:(r + 1) * 1024], AF.Silu),
                        waits=we, inc=("S", 1))
                    regTok[r] = evacTok[(fc, half)]
            evacQ = {hh: evacTok[(hh, 1)] for hh in range(2)}
            evacK = {hh: evacTok[(2 + hh, 1)] for hh in range(2)}
            evacV = {hh: evacTok[(4 + hh, 1)] for hh in range(2)}

            class _Filler:
                def __init__(self):
                    self.th = []
                    self.i = 0

                def add(self, fn):
                    self.th.append(fn)

                def pump(self, n):
                    stop = min(self.i + n, len(self.th))
                    while self.i < stop:
                        self.th[self.i]()
                        self.i += 1

                def flush(self):
                    self.pump(len(self.th))

            filler = _Filler()
            for oc in (2 * g, 2 * g + 1):
                def _dma(oc=oc):
                    wgDmaT[oc] = op("gpsimd", lambda gp, oc=oc: gp.dma_start(
                        out=wgbuf[oc % 2][:, :, :], in_=wg[oc].rearrange("p (dc f) -> p dc f", dc=16)),
                        waits=([applyTok[DCO[-1]]] if oc < 2 else [gStop[oc - 2]]), inc=("W", 16))
                filler.add(_dma)
                for r in range(2):
                    for dc in range(16):
                        def _mm(oc=oc, r=r, dc=dc):
                            w = [wgDmaT[oc], regTok[r]] if dc == 0 else []
                            for j in range(2):
                                op("tensor", lambda t, oc=oc, dc=dc, j=j, r=r: t.matmul(
                                    psA[:, r * 1024 + j * 512: r * 1024 + (j + 1) * 512],
                                    wgbuf[oc % 2][:, dc, :],
                                    xn_ap(dc)[:, r * 1024 + j * 512: r * 1024 + (j + 1) * 512],
                                    start=(dc == 0), stop=(dc == 15)),
                                    waits=(w if j == 0 else []))
                        filler.add(_mm)

                    def _evac(oc=oc, r=r):
                        pG = ptok()
                        if r == 1:
                            gStop[oc] = pG
                        we = [pG]
                        if r == 0 and oc >= 2:
                            we.append(gStore[oc - 2])
                        regTok[r] = op("scalar", lambda s_, oc=oc, r=r: s_.activation(
                            gatebuf[oc % 2][:, r * 1024:(r + 1) * 1024],
                            psA[:, r * 1024:(r + 1) * 1024], AF.Sigmoid), waits=we, inc=("S", 1))
                        gEvacH2[(oc, r)] = regTok[r]
                    filler.add(_evac)

                def _store(oc=oc):
                    gStore[oc] = op("sync", lambda s_, oc=oc: s_.dma_start(
                        out=gate_sc[oc * 128:(oc + 1) * 128, :], in_=gatebuf[oc % 2][:, :]),
                        waits=[gEvacH2[(oc, 0)], gEvacH2[(oc, 1)], gStore.get(oc - 2)], inc=("L", 16))
                filler.add(_store)

            for b in range(NB):
                c0 = b * BLK
                vq = {}
                vev = {}
                vKtl = {}
                pQk = {}
                pKvd = {}
                for hh in range(2):
                    qs = qkv_sb[:, hh, c0:c0 + BLK]
                    ks = qkv_sb[:, 2 + hh, c0:c0 + BLK]
                    vq[hh] = op("vector", lambda v, hh=hh, qs=qs, c0=c0: v.tensor_mul(
                        qt_buf[:, hh, c0:c0 + BLK], qs, qdec[:, hh, :]),
                        waits=[evacQ[hh], decTok[hh], lastIntra[hh],
                               (qtStoreTok if b == 0 else None)], inc=("V", 1))
                    w = [evacK[hh], evacV[hh], lConst, lastQkd[hh], lastIntra[hh]]
                    if g == 0 and b == 0:
                        w.append(vNmRep)
                    if g == 1 and b == 0:
                        w.append(vRt)
                    for nh in range(2):
                        op("tensor", lambda t, hh=hh, ks=ks, nh=nh: t.transpose(
                            psQKbf(hh)[:, nh * 128:(nh + 1) * 128], ks[:, nh * 128:(nh + 1) * 128], iden[:, :]),
                            waits=(w if nh == 0 else []))
                    for nh in range(2):
                        op("tensor", lambda t, hh=hh, c0=c0, nh=nh: t.transpose(
                            psQKbf(hh)[:, 256 + nh * 128: 256 + (nh + 1) * 128],
                            qkv_sb[:, 4 + hh, c0 + nh * 128: c0 + (nh + 1) * 128], iden[:, :]))
                    pTr = ptok()
                    vev[hh] = op("vector", lambda v, hh=hh: v.tensor_copy(
                        kvtok[:, hh, :, :].rearrange("p a b -> p (a b)"), psQKbf(hh)[:, 0:512]),
                        waits=[pTr], inc=("V", 1))
                    op("vector", lambda v, hh=hh: v.tensor_mul(
                        ktl[:, hh, :, :].rearrange("p a b -> p (a b)"),
                        kvtok[:, hh, 0:2, :].rearrange("p a b -> p (a b)"),
                        kdecrep[:, hh, :, :].rearrange("p a b -> p (a b)")), inc=("V", 1))
                    vKtl[hh] = tok("V")
                filler.pump(1)
                for hh in range(2):
                    w = [vKtl[hh], lastOevac[hh], lastCast[hh]]
                    if b == 0:
                        w.append(kvStoreTok[-1] if kvStoreTok else None)
                    for nh in range(2):
                        op("tensor", lambda t, hh=hh, nh=nh: t.matmul(
                            psKVd(hh), ktl[:, hh, nh, :], kvtok[:, hh, 2 + nh, :],
                            start=(nh == 0), stop=(nh == 1)),
                            waits=(w if nh == 0 else []))
                    pKvd[hh] = ptok()
                for hh in range(2):
                    if b > 0:
                        op("tensor", lambda t, hh=hh, c0=c0: t.matmul(
                            psO(hh), kvbf[:, hh, :], qt_buf[:, hh, c0:c0 + BLK], start=True, stop=False),
                            waits=[lastCast[hh], vq[hh]])
                for hh in range(2):
                    qs = qkv_sb[:, hh, c0:c0 + BLK]
                    ks = qkv_sb[:, 2 + hh, c0:c0 + BLK]
                    for nh in range(2):
                        op("tensor", lambda t, hh=hh, ks=ks, qs=qs, nh=nh: t.matmul(
                            psQK(hh)[:, nh * 256:(nh + 1) * 256], ks[:, nh * 128:(nh + 1) * 128], qs,
                            start=True, stop=True),
                            waits=([vev[hh], vq[hh]] if nh == 0 else []))
                    pQk[hh] = ptok()
                    op("vector", lambda v, hh=hh: v.tensor_tensor(
                        qkd[:, hh, :, :].rearrange("p a b -> p (a b)"), psQK(hh)[:, 0:512],
                        diagT[:, hh, :, :].rearrange("p a b -> p (a b)"),
                        ALU.mult), waits=[pQk[hh]], inc=("V", 1))
                    lastQkd[hh] = tok("V")
                filler.pump(2)
                for hh in range(2):
                    h = 2 * g + hh
                    for nh in range(2):
                        op("tensor", lambda t, hh=hh, nh=nh, b=b: t.matmul(
                            psO(hh), kvtok[:, hh, 2 + nh, :], qkd[:, hh, nh, :],
                            start=(b == 0 and nh == 0), stop=(nh == 1)),
                            waits=([lastQkd[hh]] if nh == 0 else []))
                    lastIntra[hh] = ptok()
                    lastScanPE = lastIntra[hh]
                    lastOevac[hh] = op("scalar", lambda s_, hh=hh, c0=c0: s_.activation(
                        out_loc[:, hh, c0:c0 + BLK], psO(hh), AF.Copy),
                        waits=[lastIntra[hh], (outStoreTok if b == 0 else None)], inc=("S", 1))
                    if b == 0:
                        op("vector", lambda v, hh=hh: v.tensor_copy(kvf[:, hh, :], psKVd(hh)),
                           waits=[lastIntra[hh], lastOevac[hh], (kvStoreTok[-1] if kvStoreTok else None)],
                           inc=("V", 1))
                    else:
                        op("vector", lambda v, hh=hh, h=h: v.scalar_tensor_tensor(
                            kvf[:, hh, :], kvf[:, hh, :], bd_all[:, h:h + 1], psKVd(hh), ALU.mult, ALU.add),
                            waits=[lastIntra[hh], lastOevac[hh], sConsts], inc=("V", 1))
                    vStt = tok("V")
                    lastCast[hh] = op("scalar", lambda s_, hh=hh: s_.activation(
                        kvbf[:, hh, :], kvf[:, hh, :], AF.Copy), waits=[vStt], inc=("S", 1))
                lastQtilV = vq[1]
                filler.pump(1)

            filler.flush()
            outStoreTok = op("sync", lambda s_, g=g: s_.dma_start(
                out=out_sc[2 * g * 128:(2 * g + 2) * 128, :].rearrange("(hh p) t -> p hh t", p=128),
                in_=out_loc[:, :, :]), waits=[lastOevac[0], lastOevac[1], outStoreTok], inc=("L", 16))
            qtStoreTok = op("sync", lambda s_, g=g: s_.dma_start(
                out=q_sc[2 * g * 128:(2 * g + 2) * 128, :].rearrange("(hh p) t -> p hh t", p=128),
                in_=qt_buf[:, :, :]), waits=[lastQtilV, qtStoreTok], inc=("L", 16))
            kvStoreTok.append(op("sync", lambda s_, g=g: s_.dma_start(
                out=kv_local.ap()[2 * g * 128:(2 * g + 2) * 128, :].rearrange("(hh p) e -> p hh e", p=128),
                in_=kvf[:, :, :]), waits=[lastCast[0], lastCast[1]], inc=("L", 16)))
            projUseTok[g] = lastScanPE

        op("gpsimd", lambda g: g.collective_compute(
            "AllGather", ALU.bypass, replica_groups=[list(range(NCORE))],
            ins=[kv_local.ap().opt()], outs=[kv_all.ap().opt()]),
            waits=[kvStoreTok[-1]], inc=("C", 1))
        ccTok = tok("C")

        p3Done = [x for x in (lastScanPE, lastQtilV, outStoreTok, qtStoreTok, kvStoreTok[-1],
                              lastOevac[0], lastOevac[1], lastQkd[0], lastQkd[1],
                              lastCast[0], lastCast[1], regTok[0], regTok[1],
                              gStore[14], gStore[15], gStop[15]) if x is not None]
        p15Done = p3Done

        h_sb = bview(Bbf, 0, HCN * TFF).rearrange("p (hc t) -> p hc t", t=TFF)
        wgu = {"g": bview(Bbf, 90112, 2048).rearrange("p (dc f) -> p dc f", dc=16),
               "u": bview(Bbf, 94208, 2048).rearrange("p (dc f) -> p dc f", dc=16)}
        wd_b = bview(Bbf, 98304, HCN * 128).rearrange("p (hc f) -> p hc f", hc=HCN)
        gatebf = bview(Bbf, 109568, TFF)
        upbf = bview(Bbf, 111616, TFF)
        kvin32 = bview(Bf, 113664, NH * HD).rearrange("p (h e) -> p h e", h=NH)
        kvinbf = bview(Bbf, 98304, NH * HD).rearrange("p (h e) -> p h e", h=NH)

        mmTok = {}
        evTok = {}
        hTok = {}
        dStopAll = {}
        dEvacAll = {}
        gscStore = {}
        dHalf = {}
        for tt in range(2):
            t0 = tt * TFF
            for hc in range(HCN):
                for mat, srcw, scl in (("g", fg, sg_sb), ("u", fu, su_sb)):
                    w = []
                    if (tt, hc - 1, mat) in mmTok:
                        w.append(mmTok[(tt, hc - 1, mat)])
                    elif tt > 0:
                        w.append(mmTok[(tt - 1, HCN - 1, mat)])
                    else:
                        w += p15Done
                    dmaT = op("gpsimd", lambda gp, hc=hc, srcw=srcw, mat=mat: gp.dma_start(
                        out=wgu[mat][:, :, :], in_=srcw[hc].rearrange("p (dc f) -> p dc f", dc=16)),
                        waits=w, inc=("W", 16))
                    roff = 0 if mat == "g" else 1024
                    w = [dmaT, vXnDone, vRt]
                    if (tt, hc - 1, mat) in evTok:
                        w.append(evTok[(tt, hc - 1, mat)])
                    elif tt > 0:
                        w.append(evTok[(tt - 1, HCN - 1, mat)])
                    else:
                        w += p15Done
                    for dc in range(16):
                        for j in range(2):
                            op("tensor", lambda t, hc=hc, mat=mat, dc=dc, j=j, roff=roff, t0=t0: t.matmul(
                                psA[:, roff + j * 512: roff + (j + 1) * 512],
                                wgu[mat][:, dc, :],
                                xn_ap(dc)[:, t0 + j * 512: t0 + (j + 1) * 512],
                                start=(dc == 0), stop=False),
                                waits=(w if (dc == 0 and j == 0) else []))
                    for j in range(2):
                        op("tensor", lambda t, j=j, roff=roff, t0=t0: t.matmul(
                            psA[:, roff + j * 512: roff + (j + 1) * 512], n128_r[:, :],
                            rt_bf[0:1, t0 + j * 512: t0 + (j + 1) * 512], start=False, stop=True))
                    mmTok[(tt, hc, mat)] = ptok()
                    w = [mmTok[(tt, hc, mat)], lConst]
                    if (tt, hc - 1) in hTok:
                        w.append(hTok[(tt, hc - 1)])
                    elif tt > 0:
                        w.append(hTok[(tt - 1, HCN - 1)])
                        w.append(gscStore[(tt - 1, 14)])
                        w.append(gscStore[(tt - 1, 15)])
                    func = AF.Silu if mat == "g" else AF.Copy
                    dst = gatebf if mat == "g" else upbf
                    evTok[(tt, hc, mat)] = op("scalar", lambda s_, hc=hc, scl=scl, roff=roff,
                                              func=func, dst=dst: s_.activation(
                        dst[:, :], psA[:, roff:roff + TFF], func, scale=scl[:, hc:hc + 1]),
                        waits=w, inc=("S", 1))
                hTok[(tt, hc)] = op("vector", lambda v, hc=hc: v.tensor_mul(
                    h_sb[:, hc, :], gatebf[:, :], upbf[:, :]),
                    waits=[evTok[(tt, hc, "g")], evTok[(tt, hc, "u")]], inc=("V", 1))
                if tt == 1 and exq:
                    exq.pop(0)()
            hAll = hTok[(tt, HCN - 1)]

            for j in range(2):
                for hc in range(HCN):
                    op("tensor", lambda t, j=j, hc=hc: t.matmul(
                        psA[0:1, j * 512:(j + 1) * 512], ones_c[:, :], h_sb[:, hc, j * 512:(j + 1) * 512],
                        start=(hc == 0), stop=(hc == HCN - 1)),
                        waits=([hAll, evTok[(tt, HCN - 1, "g")], evTok[(tt, HCN - 1, "u")]]
                               if (j == 0 and hc == 0) else []))
            pRh = ptok()
            vRh = op("vector", lambda v: v.tensor_copy(rh_bf[:, :], psA[0:1, 0:TFF]), waits=[pRh], inc=("V", 1))

            dStop = {}
            dEvac = {}
            HSPL = 22
            for fc in range(16):
                wA = []
                if (tt, fc - 1) in dHalf:
                    wA.append(dHalf[(tt, fc - 1)])
                elif tt > 0:
                    wA.append(dHalf[(tt - 1, 15)])
                else:
                    wA += p15Done
                fdA = op("gpsimd", lambda gp, fc=fc: gp.dma_start(
                    out=wd_b[:, 0:HSPL, :], in_=fd[fc].rearrange("p (hc f) -> p hc f", hc=HCN)[:, 0:HSPL, :]),
                    waits=wA, inc=("W", 16))
                wB = []
                if fc >= 1:
                    wB.append(dStop[fc - 1])
                elif tt > 0:
                    wB.append(dStopAll[(tt - 1, 15)])
                else:
                    wB += p15Done
                fdB = op("gpsimd", lambda gp, fc=fc: gp.dma_start(
                    out=wd_b[:, HSPL:HCN, :], in_=fd[fc].rearrange("p (hc f) -> p hc f", hc=HCN)[:, HSPL:HCN, :]),
                    waits=wB, inc=("W", 16))
                reg_off = (fc % 2) * 1024
                w = [fdA, hAll, vRh]
                if fc >= 2:
                    w.append(dEvac[fc - 2])
                elif tt > 0:
                    w.append(dEvacAll[(tt - 1, 14 + fc)])
                for hc in range(HCN):
                    for j in range(2):
                        ww = w if (hc == 0 and j == 0) else ([fdB] if (hc == HSPL and j == 0) else [])
                        op("tensor", lambda t, fc=fc, hc=hc, j=j, reg_off=reg_off: t.matmul(
                            psB[:, reg_off + j * 512: reg_off + (j + 1) * 512],
                            wd_b[:, hc, :], h_sb[:, hc, j * 512:(j + 1) * 512],
                            start=(hc == 0), stop=False),
                            waits=ww)
                    if hc == HSPL - 1:
                        dHalf[(tt, fc)] = ptok()
                for j in range(2):
                    op("tensor", lambda t, j=j, reg_off=reg_off: t.matmul(
                        psB[:, reg_off + j * 512: reg_off + (j + 1) * 512], n128_r[:, :],
                        rh_bf[0:1, j * 512:(j + 1) * 512], start=False, stop=True))
                dStop[fc] = ptok()
                dStopAll[(tt, fc)] = dStop[fc]
                w = [dStop[fc]]
                if fc >= 2:
                    w.append(gscStore[(tt, fc - 2)])
                elif tt > 0:
                    w.append(gscStore[(tt - 1, 14 + fc)])
                dbuf = gatebf if fc % 2 == 0 else upbf
                dEvac[fc] = op("scalar", lambda s_, fc=fc, reg_off=reg_off, dbuf=dbuf: s_.activation(
                    dbuf[:, :], psB[:, reg_off:reg_off + TFF], AF.Silu, scale=sd_sb[:, fc:fc + 1]),
                    waits=w, inc=("S", 1))
                dEvacAll[(tt, fc)] = dEvac[fc]
                gscStore[(tt, fc)] = op("sync", lambda s_, fc=fc, t0=t0, dbuf=dbuf: s_.dma_start(
                    out=g_sc[fc * 128:(fc + 1) * 128, t0:t0 + TFF], in_=dbuf[:, :]),
                    waits=[dEvac[fc]], inc=("L", 16))
            if tt == 0:
                for h in range(NH):
                    op("scalar", lambda s_, h=h: s_.activation(
                        scx[:, h, :], mrep[:, :], AF.Exp, scale=neg_s[:, h:h + 1]),
                        waits=([lConst, vSrep] if h == 0 else []), inc=("S", 1))
                sScx = tok("S")
                exq = []
                exstate = {"stt": None}

                def _expkt(cp, hq):
                    w = [ccTok, exstate["stt"]]
                    kvLoad = op("sync", lambda s_, cp=cp, hq=hq: s_.dma_start(
                        out=kvc[:, :, :],
                        in_=kv_all.ap()[cp * 2048 + hq * 512: cp * 2048 + (hq + 1) * 512, :]
                        .rearrange("(h p) e -> p h e", p=128)),
                        waits=w, inc=("L", 16))
                    for h4 in range(4):
                        h = hq * 4 + h4
                        if cp == 0:
                            op("vector", lambda v, h=h, h4=h4, cp=cp: v.tensor_scalar_mul(
                                kvin32[:, h, :], kvc[:, h4, :], scx[:, h, cp:cp + 1]),
                                waits=([kvLoad, sScx] if h4 == 0 else []), inc=("V", 1))
                        else:
                            op("vector", lambda v, h=h, h4=h4, cp=cp: v.scalar_tensor_tensor(
                                kvin32[:, h, :], kvc[:, h4, :], scx[:, h, cp:cp + 1],
                                kvin32[:, h, :], ALU.mult, ALU.add),
                                waits=([kvLoad] if h4 == 0 else []), inc=("V", 1))
                    exstate["stt"] = tok("V")

                for cp in range(NCORE):
                    for hq in range(4):
                        exq.append((lambda cp=cp, hq=hq: _expkt(cp, hq)))
        while exq:
            exq.pop(0)()
        vKvinSum = exstate["stt"]
        vKvin = None
        p2Done = [dStopAll[(1, 15)], dEvacAll[(1, 15)], gscStore[(1, 14)], gscStore[(1, 15)],
                  hTok[(1, HCN - 1)], mmTok[(1, HCN - 1, "g")], mmTok[(1, HCN - 1, "u")],
                  evTok[(1, HCN - 1, "g")], evTok[(1, HCN - 1, "u")]]

        vKvin = op("vector", lambda v: v.tensor_copy(
            kvinbf[:, :, :].rearrange("p a b -> p (a b)"),
            kvin32[:, :, :].rearrange("p a b -> p (a b)")), waits=[vKvinSum] + list(p2Done), inc=("V", 1))
        if debug:
            op("gpsimd", lambda g: g.dma_start(
                out=dbg_kvin, in_=kvin32[:, :, :].rearrange("p a b -> p (a b)")),
                waits=[vKvin], inc=("W", 16))

        olc = [bview(Bbf, 32768, NH * 256).rearrange("p (h t) -> p h t", h=NH),
               bview(Bbf, 40960, NH * 256).rearrange("p (h t) -> p h t", h=NH)]
        qts = [bview(Bbf, 49152, NH * 256).rearrange("p (h t) -> p h t", h=NH),
               bview(Bbf, 57344, NH * 256).rearrange("p (h t) -> p h t", h=NH)]
        gts = [bview(Bbf, 65536, NH * 256).rearrange("p (h t) -> p h t", h=NH),
               bview(Bbf, 73728, NH * 256).rearrange("p (h t) -> p h t", h=NH)]
        oco = [bview(Bbf, 81920, NH * 256).rearrange("p (h t) -> p h t", h=NH),
               bview(Bbf, 90112, NH * 256).rearrange("p (h t) -> p h t", h=NH)]
        row_ss5 = [bview(Bf, 102400, 256)[0:1, :], bview(Bf, 103424, 256)[0:1, :]]
        row_rstd5 = [bview(Bbf, 104448, 256)[0:1, :], bview(Bbf, 104960, 256)[0:1, :]]
        sqt = [bview(Bbf, 105472, 256), bview(Bbf, 105984, 256),
               bview(Bbf, 114688, 256), bview(Bbf, 115200, 256)]
        utmp = [bview(Bf, 106496, 256), bview(Bf, 107520, 256)]
        nwrow = bview(Bbf, 108544, D)[0:1, :]

        nwTok = op("gpsimd", lambda g: g.dma_start(out=nwrow[:, :], in_=nw), waits=list(p2Done), inc=("W", 16))

        NSL = 8
        sl_u2last = {}
        sl_sqlast = {}
        sl_corrPE = {}
        sl_u1last = {}
        sl_sss = {}
        sl_vrec = {}
        icorr = 0
        corrV = {}
        prevOuterPE = {}
        for sl in range(NSL):
            par = sl % 2
            qc = sl * 256
            ldTok = []
            for dram, dst in ((out_sc, olc), (q_sc, qts), (gate_sc, gts)):
                for i in range(2):
                    w = []
                    if i == 0 and dram is out_sc:
                        if sl == 0:
                            w = list(p2Done) + [vKvin]
                        elif sl == 1:
                            w = [vKvin]
                        else:
                            w = [sl_u1last[sl - 2], sl_sqlast[sl - 2], sl_corrPE[sl - 2]]
                    ldTok.append(op("sync", lambda s_, dram=dram, dst=dst, qc=qc, i=i, par=par: s_.dma_start(
                        out=dst[par][:, 8 * i:8 * i + 8, :],
                        in_=dram[:, qc:qc + 256].rearrange("(h p) t -> p h t", p=128)[:, 8 * i:8 * i + 8, :]),
                        waits=w, inc=("L", 16)))

            for h in range(NH):
                roff = (icorr % 4) * 512
                w = [vKvin, ldTok, corrV.get(icorr - 4)]
                if sl == 0 and h < 4:
                    w += list(p2Done)
                op("tensor", lambda t, h=h, roff=roff, par=par: t.matmul(
                    psA[:, roff:roff + 256], kvinbf[:, h, :], qts[par][:, h, :],
                    start=True, stop=True), waits=w)
                pC = ptok()
                if h == NH - 1:
                    sl_corrPE[sl] = pC
                w = [pC, sConsts]
                if h == 0 and sl >= 2:
                    w.append(sl_sqlast[sl - 2])
                corrV[icorr] = op("vector", lambda v, h=h, sl=sl, par=par, roff=roff: v.scalar_tensor_tensor(
                    oco[par][:, h, :], psA[:, roff:roff + 256], bdg[:, sl, h:h + 1],
                    olc[par][:, h, :], ALU.mult, ALU.add), waits=w, inc=("V", 1))
                icorr += 1

            ssreg = psB[0:1, 0:256] if par == 0 else psB[0:1, 1536:1792]
            for h in range(NH):
                if h >= 4:
                    prd = prevOuterPE.get((sl, "sq", h - 4))
                else:
                    prd = prevOuterPE.get((sl - 1, "sq", NH - 4 + h))
                w = [corrV[icorr - NH + h], prd]
                w = [x for x in w if x is not None]
                sq = op("scalar", lambda s_, h=h, par=par: s_.activation(
                    sqt[h % 4][:, :], oco[par][:, h, :], AF.Square), waits=w, inc=("S", 1))
                if h == NH - 1:
                    sl_sqlast[sl] = sq
                ww = [sq]
                if h == 0 and sl >= 2:
                    ww.append(sl_sss[sl - 2])
                op("tensor", lambda t, h=h, ssreg=ssreg: t.matmul(
                    ssreg, ones_c[:, :], sqt[h % 4][:, :], start=(h == 0), stop=(h == NH - 1)),
                    waits=ww)
                prevOuterPE[(sl, "sq", h)] = ptok()
            sl_sss[sl] = op("scalar", lambda s_, ssreg=ssreg, par=par: s_.activation(
                row_ss5[par][:, :], ssreg, AF.Sqrt, scale=1.0 / D, bias=eps_rms[:, :]),
                waits=[prevOuterPE[(sl, "sq", NH - 1)]], inc=("S", 1))
            w = [sl_sss[sl]]
            if sl >= 2:
                w.append(sl_u2last_pe)
            op("vector", lambda v, par=par: v.reciprocal(utmp[0][0:1, :], row_ss5[par][:, :]),
               waits=w, inc=("V", 1))
            sl_vrec[sl] = op("vector", lambda v, par=par: v.tensor_copy(row_rstd5[par][:, :], utmp[0][0:1, :]),
                             inc=("V", 1))

            u2T = {}
            for oc in range(16):
                roff = 512 + (oc % 2) * 512
                w = [sl_vrec[sl], nwTok, u2T.get(oc - 2)]
                if oc < 2 and sl >= 1:
                    w.append(sl_u2last.get(sl - 1))
                op("tensor", lambda t, oc=oc, roff=roff, par=par: t.matmul(
                    psB[:, roff:roff + 256], nwrow[0:1, oc * 128:(oc + 1) * 128], row_rstd5[par][:, :],
                    start=True, stop=True), waits=w)
                pOut = ptok()
                u1 = op("vector", lambda v, oc=oc, par=par: v.tensor_mul(
                    utmp[oc % 2][:, :], oco[par][:, oc, :], gts[par][:, oc, :]),
                    waits=[corrV[icorr - NH + oc], ldTok[-1]], inc=("V", 1))
                u2T[oc] = op("vector", lambda v, oc=oc, qc=qc, roff=roff: v.tensor_mul(
                    u_ap(oc)[:, qc:qc + 256], utmp[oc % 2][:, :], psB[:, roff:roff + 256]),
                    waits=[pOut], inc=("V", 1))
            sl_u1last[sl] = u1
            sl_u2last[sl] = u2T[15]
            sl_u2last_pe = pOut
        uDone = sl_u2last[NSL - 1]

        wo_b = [bview(Bbf, 32768, 2048).rearrange("p (oc f) -> p oc f", oc=16),
                bview(Bbf, 36864, 2048).rearrange("p (oc f) -> p oc f", oc=16)]
        gbf6 = [bview(Bbf, 40960, T), bview(Bbf, 45056, T)]
        xc6 = [bview(Bf, 49152, T), bview(Bf, 57344, T)]
        osb = [bview(Bf, 65536, T), bview(Bf, 73728, T)]
        tmp6 = [bview(Bf, 81920, T), bview(Bf, 90112, T)]

        mmStop = {}
        p1Tok = {}
        p2Tok = {}
        stTok = {}
        for fc in range(16):
            w = [uDone] if fc < 2 else [mmStop[fc - 2]]
            woT = op("gpsimd", lambda gp, fc=fc: gp.dma_start(
                out=wo_b[fc % 2][:, :, :], in_=wo[fc].rearrange("p (oc f) -> p oc f", oc=16)),
                waits=w, inc=("W", 16))
            w = [uDone] if fc < 2 else [p1Tok[fc - 2]]
            gl = op("sync", lambda s_, fc=fc: s_.dma_start(
                out=gbf6[fc % 2][:, :], in_=g_sc[fc * 128:(fc + 1) * 128, :]), waits=w, inc=("L", 16))
            xl = op("sync", lambda s_, fc=fc: s_.dma_start(
                out=xc6[fc % 2][:, :], in_=xt[fc * 128:(fc + 1) * 128, :]),
                waits=([uDone] if fc < 2 else [p2Tok[fc - 2]]), inc=("L", 16))
            reg = psA if fc % 2 == 0 else psB
            w = [woT, uDone]
            if fc >= 2:
                w.append(p1Tok[fc - 2])
            for oc in range(16):
                for j in range(4):
                    op("tensor", lambda t, fc=fc, oc=oc, j=j, reg=reg: t.matmul(
                        reg[:, j * 512:(j + 1) * 512], wo_b[fc % 2][:, oc, :],
                        u_ap(oc)[:, j * 512:(j + 1) * 512], start=(oc == 0), stop=(oc == 15)),
                        waits=(w if (oc == 0 and j == 0) else [])),
            mmStop[fc] = ptok()
            p1Tok[fc] = op("vector", lambda v, fc=fc, reg=reg: v.tensor_mul(
                tmp6[fc % 2][:, :], reg[:, :], gbf6[fc % 2][:, :]), waits=[mmStop[fc], gl], inc=("V", 1))
            p2Tok[fc] = op("vector", lambda v, fc=fc: v.tensor_add(
                osb[fc % 2][:, :], tmp6[fc % 2][:, :], xc6[fc % 2][:, :]),
                waits=[xl, stTok.get(fc - 2)], inc=("V", 1))
            stTok[fc] = op("sync", lambda s_, fc=fc: s_.dma_start(
                out=out_d[fc * 128:(fc + 1) * 128, :], in_=osb[fc % 2][:, :]), waits=[p2Tok[fc]], inc=("O", 16))

        for (s, v) in qtoks("O"):
            op("sync", lambda s_, s0=s, v0=v: s_.wait_ge(sem[s0], v0))

        block = e(nc.Block())

        SELF_SER = {"vector": "V", "scalar": "S"}

        def run_engine(eng_obj, name):
            own = SELF_SER.get(name)
            nown = 0
            for (engname, ws, fn, inc) in ops:
                if engname != name:
                    continue
                if own is not None and nown > 0:
                    eng_obj.wait_ge(sem[own], nown)
                for (s, v) in ws:
                    eng_obj.wait_ge(sem[s], v)
                ins = fn(eng_obj)
                if inc is not None and ins is not None:
                    ins.then_inc(sem[inc[0]], inc[1])
                    if own is not None and inc[0] == own:
                        nown += inc[1]

        block.sync(lambda s_: run_engine(s_, "sync"))
        block.gpsimd(lambda g: run_engine(g, "gpsimd"))
        block.tensor(lambda t: run_engine(t, "tensor"))
        block.vector(lambda v: run_engine(v, "vector"))
        block.scalar(lambda s_: run_engine(s_, "scalar"))

    return nc


def _prep_inputs(inputs):
    x = np.asarray(inputs["L0_attn_hidden_states"])
    wqkv = np.asarray(inputs["L0_attn_w_qkv"])
    wgate = np.asarray(inputs["L0_attn_w_gate"])
    wout = np.asarray(inputs["L0_attn_w_out"])
    normw = np.asarray(inputs["L0_attn_norm_w"])
    slope = np.asarray(inputs["L0_attn_slope_rate"])
    fgq = np.asarray(inputs["L0_ffn_wq_gate"])
    fuq = np.asarray(inputs["L0_ffn_wq_up"])
    fdq = np.asarray(inputs["L0_ffn_wq_down"])
    sgv = np.asarray(inputs["L0_ffn_s_gate"])
    suv = np.asarray(inputs["L0_ffn_s_up"])
    sdv = np.asarray(inputs["L0_ffn_s_down"])

    xT = np.ascontiguousarray(x[0].T)

    wq3 = wqkv.reshape(16, 3, 128, 2048)
    blocks = []
    for g in range(8):
        rows = np.concatenate([wq3[2 * g, 0], wq3[2 * g + 1, 0],
                               wq3[2 * g, 1], wq3[2 * g + 1, 1],
                               wq3[2 * g, 2], wq3[2 * g + 1, 2]], axis=0)
        blocks.append(rows.reshape(6, 128, 16, 128).transpose(3, 0, 2, 1))
    wqkv_b = np.ascontiguousarray(np.stack(blocks)).reshape(8, 128, 6 * 16 * 128)

    wg_b = np.ascontiguousarray(
        wgate.reshape(16, 128, 16, 128).transpose(0, 3, 2, 1)).reshape(16, 128, 16 * 128)
    wo_b = np.ascontiguousarray(
        wout.reshape(16, 128, 16, 128).transpose(0, 3, 2, 1)).reshape(16, 128, 16 * 128)
    fg_b = np.ascontiguousarray(
        fgq.reshape(44, 128, 16, 128).transpose(0, 3, 2, 1)).reshape(44, 128, 16 * 128)
    fu_b = np.ascontiguousarray(
        fuq.reshape(44, 128, 16, 128).transpose(0, 3, 2, 1)).reshape(44, 128, 16 * 128)
    fd_b = np.ascontiguousarray(
        fdq.reshape(16, 128, 44, 128).transpose(0, 3, 2, 1)).reshape(16, 128, 44 * 128)

    sg_l = np.ascontiguousarray(sgv.reshape(44, 128).T)
    su_l = np.ascontiguousarray(suv.reshape(44, 128).T)
    sd_l = np.ascontiguousarray(sdv.reshape(16, 128).T)
    nw_l = np.ascontiguousarray(normw.reshape(1, 2048))
    sr_l = np.ascontiguousarray(slope.reshape(1, 16))

    in_maps = []
    for c in range(NCORE):
        m = np.full((1, NCORE), 1e9, dtype=np.float32)
        for cp in range(c):
            m[0, cp] = 2048.0 * (c - 1 - cp)
        in_maps.append(dict(
            xt=np.ascontiguousarray(xT[:, c * T:(c + 1) * T]),
            wqkv=wqkv_b, wg=wg_b, wo=wo_b, fg=fg_b, fu=fu_b, fd=fd_b,
            sg=sg_l, su=su_l, sd=sd_l, nw=nw_l, sr=sr_l, mrow=m))
    return in_maps


def kernel(**inputs):
    in_maps = _prep_inputs(inputs)
    nc = build(debug=False)
    res = run_bass_kernel_spmd(nc, in_maps, core_ids=list(range(NCORE)))
    outT = np.concatenate([r["out"] for r in res.results], axis=1)
    return np.ascontiguousarray(outT.T)[None].astype(np.float32)


# version 17
# speedup vs baseline: 1.0194x; 1.0044x over previous
import numpy as np
import ml_dtypes
from contextlib import ExitStack

import concourse.bass as bass
import concourse.mybir as mybir
from concourse.bass_utils import run_bass_kernel_spmd

F32 = mybir.dt.float32
BF16 = mybir.dt.bfloat16
I32 = mybir.dt.int32
U8 = mybir.dt.uint8
AF = mybir.ActivationFunctionType
ALU = mybir.AluOpType

NCORE = 8
D = 2048
T = 2048
NH = 16
HD = 128
BLK = 256
NB = T // BLK
NG = 8
HID = 5632
HCN = 44
TFF = 1024
LN_EPS = 1e-5
RMS_EPS = 1e-6


def _bf16_bits(x):
    return np.ascontiguousarray(x.astype(ml_dtypes.bfloat16).view(np.uint16))


def build(debug=False):
    nc = bass.Bass("TRN2", target_bir_lowering=False)

    xt = nc.dram_tensor("xt", [D, T], F32, kind="ExternalInput").ap()
    wqkv = nc.dram_tensor("wqkv", [NG, 128, 6 * 16 * 128], F32, kind="ExternalInput").ap()
    wg = nc.dram_tensor("wg", [16, 128, 16 * 128], F32, kind="ExternalInput").ap()
    wo = nc.dram_tensor("wo", [16, 128, 16 * 128], F32, kind="ExternalInput").ap()
    fg = nc.dram_tensor("fg", [HCN, 128, 16 * 128], I32, kind="ExternalInput").ap()
    fu = nc.dram_tensor("fu", [HCN, 128, 16 * 128], I32, kind="ExternalInput").ap()
    fd = nc.dram_tensor("fd", [16, 128, HCN * 128], I32, kind="ExternalInput").ap()
    sg = nc.dram_tensor("sg", [128, HCN], F32, kind="ExternalInput").ap()
    su = nc.dram_tensor("su", [128, HCN], F32, kind="ExternalInput").ap()
    sd = nc.dram_tensor("sd", [128, 16], F32, kind="ExternalInput").ap()
    nw = nc.dram_tensor("nw", [1, D], F32, kind="ExternalInput").ap()
    sr = nc.dram_tensor("sr", [1, NH], F32, kind="ExternalInput").ap()
    mrow = nc.dram_tensor("mrow", [1, NCORE], F32, kind="ExternalInput").ap()

    out_d = nc.dram_tensor("out", [D, T], F32, kind="ExternalOutput").ap()

    ikind = "ExternalOutput" if debug else "Internal"
    if debug == "kinds":
        debug = False
        ikind = "ExternalOutput"
    gate_sc = nc.dram_tensor("gate_sc", [D, T], BF16, kind=ikind).ap()
    out_sc = nc.dram_tensor("out_sc", [D, T], BF16, kind=ikind).ap()
    q_sc = nc.dram_tensor("q_sc", [D, T], BF16, kind=ikind).ap()
    g_sc = nc.dram_tensor("g_sc", [D, T], BF16, kind=ikind).ap()
    kv_local = nc.dram_tensor("kv_local", [NH * HD, HD], F32, kind="Internal")
    kv_all = nc.dram_tensor("kv_all", [NCORE * NH * HD, HD], F32, kind="Internal", addr_space="Shared")
    if debug:
        dbg_xn = nc.dram_tensor("dbg_xn", [128, 16 * T], F32, kind="ExternalOutput").ap()
        dbg_rt = nc.dram_tensor("dbg_rt", [1, T], F32, kind="ExternalOutput").ap()
        dbg_kvin = nc.dram_tensor("dbg_kvin", [128, NH * HD], F32, kind="ExternalOutput").ap()

    iden_c = nc.inline_tensor(_bf16_bits(np.eye(128, dtype=np.float32)), name="idenb")
    arr_c = nc.inline_tensor(_bf16_bits(
        np.broadcast_to(np.arange(1, BLK + 1, dtype=np.float32), (128, BLK)).copy()), name="arrrep")
    n_idx = (np.arange(128)[:, None, None] + 128 * np.arange(2)[None, :, None]).astype(np.float32)
    m_idx = np.broadcast_to(np.arange(BLK, dtype=np.float32), (128, 2, BLK))
    idxT_np = np.where(m_idx >= n_idx, m_idx - n_idx, np.float32(1e9)).astype(np.float32)
    idxT_c = nc.inline_tensor(_bf16_bits(np.ascontiguousarray(idxT_np)), name="idxT")
    karr_np = np.broadcast_to(
        (255.0 - (np.arange(128)[:, None, None] + 128 * np.arange(2)[None, :, None])), (128, 2, 128)
    ).astype(np.float32)
    karr_c = nc.inline_tensor(_bf16_bits(np.ascontiguousarray(karr_np)), name="karr")

    with ExitStack() as stk:
        e = stk.enter_context
        iden = e(nc.sbuf_tensor("iden", [128, 128], BF16))
        ones_c = e(nc.sbuf_tensor("ones_c", [128, 1], BF16))
        ones_rf = e(nc.sbuf_tensor("ones_rf", [1, 128], F32))
        n128_r = e(nc.sbuf_tensor("n128_r", [1, 128], BF16))
        srow = e(nc.sbuf_tensor("srow", [1, NH], F32))
        s_rep = e(nc.sbuf_tensor("s_rep", [128, NH], F32))
        neg_s = e(nc.sbuf_tensor("neg_s", [128, NH], F32))
        bd_all = e(nc.sbuf_tensor("bd_all", [128, NH], F32))
        bdg = e(nc.sbuf_tensor("bdg", [128, NB, NH], F32))
        mrep = e(nc.sbuf_tensor("mrep", [128, NCORE], F32))
        scx = e(nc.sbuf_tensor("scx", [128, NH, NCORE], F32))
        sg_sb = e(nc.sbuf_tensor("sg_sb", [128, HCN], F32))
        su_sb = e(nc.sbuf_tensor("su_sb", [128, HCN], F32))
        sd_sb = e(nc.sbuf_tensor("sd_sb", [128, 16], F32))
        arr_sb = e(nc.sbuf_tensor("arr_sb", [128, BLK], BF16))
        idxT_sb = e(nc.sbuf_tensor("idxT_sb", [128, 2, BLK], BF16))
        karr_sb = e(nc.sbuf_tensor("karr_sb", [128, 2, 128], BF16))
        rt_bf = e(nc.sbuf_tensor("rt_bf", [1, T], BF16))
        rh_bf = e(nc.sbuf_tensor("rh_bf", [1, TFF], BF16))
        qdec = e(nc.sbuf_tensor("qdec", [128, 2, BLK], BF16))
        diagT = e(nc.sbuf_tensor("diagT", [128, 2, 2, BLK], BF16))
        kdecrep = e(nc.sbuf_tensor("kdecrep", [128, 2, 2, 128], BF16))
        kvf = e(nc.sbuf_tensor("kvf", [128, 2, HD], F32))
        kvbf = e(nc.sbuf_tensor("kvbf", [128, 2, HD], BF16))
        kvtok = e(nc.sbuf_tensor("kvtok", [128, 2, 4, 128], BF16))
        ktl = e(nc.sbuf_tensor("ktl", [128, 2, 2, 128], BF16))
        qkd = e(nc.sbuf_tensor("qkd", [128, 2, 2, BLK], BF16))
        eps_ln = e(nc.sbuf_tensor("eps_ln", [1, 1], F32))
        kvc = e(nc.sbuf_tensor("kvc", [128, 4, HD], F32))
        eps_rms = e(nc.sbuf_tensor("eps_rms", [1, 1], F32))

        Abuf = e(nc.sbuf_tensor("Abuf", [128, 65536], U8))
        Bbuf = e(nc.sbuf_tensor("Bbuf", [128, 121856], U8))
        Abf = Abuf.bitcast(BF16).ap()

        def xn_ap(dc):
            return Abf[:, dc * T:(dc + 1) * T]
        u_ap = xn_ap

        Bbf = Bbuf.bitcast(BF16).ap()
        Bf = Bbuf.bitcast(F32).ap()

        def bview(dt_ap, byte_off, n):
            esz = 2 if dt_ap is Bbf else 4
            assert byte_off % esz == 0
            return dt_ap[:, byte_off // esz: byte_off // esz + n]

        psA = e(nc.psum_tensor("psA", [128, 2048], F32))
        psB = e(nc.psum_tensor("psB", [128, 2048], F32))

        POOL = {"L": 6, "W": 6, "O": 2}
        sem = {}
        for s in ["P", "V", "S", "C"]:
            sem[s] = e(nc.semaphore(s))
        for q, n in POOL.items():
            for i in range(n):
                sem[f"{q}{i}"] = e(nc.semaphore(f"{q}{i}"))
        cnt = {s: 0 for s in sem}
        pool_idx = {q: 0 for q in POOL}
        OWN = {"P": "tensor", "V": "vector", "S": "scalar"}
        ops = []

        def op(eng, fn, waits=(), inc=None):
            flat = []
            for wt in waits:
                if wt is None:
                    continue
                if isinstance(wt, list):
                    flat.extend(x for x in wt if x is not None)
                else:
                    flat.append(wt)
            if inc is not None and inc[0] in POOL:
                q = inc[0]
                i = pool_idx[q]
                pool_idx[q] = (i + 1) % POOL[q]
                name = f"{q}{i}"
                if cnt[name] > 0:
                    flat.append((name, cnt[name]))
                inc = (name, 16)
            ws = []
            for (s, v) in flat:
                if v <= 0 or (s in OWN and OWN[s] == eng):
                    continue
                assert v <= cnt[s], f"wait on future token {s}={v} (now {cnt[s]})"
                if (s, v) not in ws:
                    ws.append((s, v))
            ops.append((eng, ws, fn, inc))
            if inc is not None:
                cnt[inc[0]] += inc[1]
                return (inc[0], cnt[inc[0]])
            return None

        def tok(s):
            return (s, cnt[s])

        def qtoks(q):
            return [(f"{q}{i}", cnt[f"{q}{i}"]) for i in range(POOL[q]) if cnt[f"{q}{i}"] > 0]

        _last_ptok = [None]

        def ptok():
            for i in range(len(ops) - 1, -1, -1):
                if ops[i][0] == "tensor":
                    if ops[i][3] is not None:
                        return _last_ptok[0]
                    ops[i] = (ops[i][0], ops[i][1], ops[i][2], ("P", 1))
                    break
            else:
                raise AssertionError("no tensor op to attach token to")
            cnt["P"] += 1
            _last_ptok[0] = ("P", cnt["P"])
            return _last_ptok[0]

        op("sync", lambda s_: s_.dma_start(out=iden[:, :], in_=iden_c.ap().bitcast(BF16)), inc=("L", 16))
        op("sync", lambda s_: s_.dma_start(out=arr_sb[:, :], in_=arr_c.ap().bitcast(BF16)), inc=("L", 16))
        op("sync", lambda s_: s_.dma_start(out=idxT_sb[:, :, :], in_=idxT_c.ap().bitcast(BF16)), inc=("L", 16))
        op("sync", lambda s_: s_.dma_start(out=karr_sb[:, :, :], in_=karr_c.ap().bitcast(BF16)), inc=("L", 16))
        op("sync", lambda s_: s_.dma_start(out=srow[:, :], in_=sr), inc=("L", 16))
        op("sync", lambda s_: s_.dma_start(out=sg_sb[:, :], in_=sg), inc=("L", 16))
        op("sync", lambda s_: s_.dma_start(out=su_sb[:, :], in_=su), inc=("L", 16))
        op("sync", lambda s_: s_.dma_start(out=sd_sb[:, :], in_=sd), inc=("L", 16))
        op("sync", lambda s_: s_.dma_start(out=mrep[:, :], in_=mrow.broadcast_to([128, NCORE])), inc=("L", 16))
        lConst = qtoks("L")

        op("vector", lambda v: v.memset(ones_c[:, :], 1.0), inc=("V", 1))
        op("vector", lambda v: v.memset(ones_rf[:, :], 1.0), inc=("V", 1))
        op("vector", lambda v: v.memset(n128_r[:, :], -128.0), inc=("V", 1))
        op("vector", lambda v: v.memset(eps_ln[:, :], LN_EPS), inc=("V", 1))
        op("vector", lambda v: v.memset(eps_rms[:, :], RMS_EPS), inc=("V", 1))
        vMemset = tok("V")

        op("tensor", lambda t: t.matmul(psB[:, 0:NH], ones_rf[:, :], srow[:, :], start=True, stop=True),
           waits=[lConst, vMemset])
        pSrep = ptok()
        op("vector", lambda v: v.tensor_copy(s_rep[:, :], psB[:, 0:NH]), waits=[pSrep], inc=("V", 1))
        op("vector", lambda v: v.tensor_scalar_mul(neg_s[:, :], s_rep[:, :], -1.0), inc=("V", 1))
        vSrep = tok("V")
        op("scalar", lambda s_: s_.activation(bd_all[:, :], s_rep[:, :], AF.Exp, scale=-256.0),
           waits=[vSrep], inc=("S", 1))
        for b in range(NB):
            op("scalar", lambda s_, b=b: s_.activation(bdg[:, b, :], s_rep[:, :], AF.Exp, scale=-256.0 * b),
               inc=("S", 1))
        sConsts = tok("S")

        XBF = bview(Bbf, 0, 16 * T).rearrange("p (dc t) -> p dc t", t=T)
        SQ = [bview(Bbf, 65536, T), bview(Bbf, 69632, T)]
        row_mean = bview(Bf, 73728, T)[0:1, :]
        row_b = bview(Bf, 81920, T)[0:1, :]
        tmpP1 = bview(Bf, 90112, T)
        std_row = tmpP1[0:1, :]
        rstd_rep = bview(Bf, 98304, T)
        nm_rep = bview(Bf, 106496, T)

        xbf_dma = []
        for i in range(8):
            xbf_dma.append(op("gpsimd", lambda g, i=i: g.dma_start(
                out=XBF[:, 2 * i:2 * i + 2, :],
                in_=xt.rearrange("(dc p) t -> p dc t", p=128)[:, 2 * i:2 * i + 2, :]), inc=("W", 16)))

        statsTok = []
        for dc in range(16):
            wq_ = [xbf_dma[dc // 2]]
            if dc >= 2:
                wq_.append(statsTok[dc - 2])
            sqT = op("vector", lambda v, dc=dc: v.tensor_mul(SQ[dc % 2][:, :], XBF[:, dc, :], XBF[:, dc, :]),
                     waits=wq_, inc=("V", 1))
            w = [xbf_dma[dc // 2], vMemset, vSrep]
            for j in range(4):
                op("tensor", lambda t, dc=dc, j=j: t.matmul(
                    psB[0:1, j * 512:(j + 1) * 512], ones_c[:, :], XBF[:, dc, j * 512:(j + 1) * 512],
                    start=(dc == 0), stop=(dc == 15)), waits=(w if j == 0 else []))
            for j in range(4):
                op("tensor", lambda t, dc=dc, j=j: t.matmul(
                    psA[0:1, j * 512:(j + 1) * 512], ones_c[:, :], SQ[dc % 2][:, j * 512:(j + 1) * 512],
                    start=(dc == 0), stop=(dc == 15)), waits=([sqT] if j == 0 else []))
            statsTok.append(ptok())
        pStats = statsTok[-1]

        op("scalar", lambda s_: s_.activation(row_mean[:, :], psB[0:1, :], AF.Copy, scale=1.0 / D),
           waits=[pStats], inc=("S", 1))
        op("scalar", lambda s_: s_.activation(row_b[:, :], psB[0:1, :], AF.Square, scale=1.0 / D),
           inc=("S", 1))
        sRows1 = tok("S")
        op("vector", lambda v: v.scalar_tensor_tensor(row_b[:, :], psA[0:1, :], 1.0 / D, row_b[:, :],
                                                      ALU.mult, ALU.subtract), waits=[sRows1], inc=("V", 1))
        vVar = tok("V")
        op("scalar", lambda s_: s_.activation(std_row[:, :], row_b[:, :], AF.Sqrt, bias=eps_ln[:, :]),
           waits=[vVar], inc=("S", 1))
        sStd = tok("S")
        op("vector", lambda v: v.reciprocal(row_b[:, :], std_row[:, :]), waits=[sStd], inc=("V", 1))
        op("vector", lambda v: v.scalar_tensor_tensor(row_mean[:, :], row_mean[:, :], -1.0, row_b[:, :],
                                                      ALU.mult, ALU.mult), inc=("V", 1))
        vRowsDone = tok("V")

        for j in range(4):
            op("tensor", lambda t, j=j: t.matmul(psA[:, j * 512:(j + 1) * 512], ones_rf[:, :],
                                                 row_b[0:1, j * 512:(j + 1) * 512], start=True, stop=True),
               waits=([vRowsDone] if j == 0 else []))
        for j in range(4):
            op("tensor", lambda t, j=j: t.matmul(psB[:, j * 512:(j + 1) * 512], ones_rf[:, :],
                                                 row_mean[0:1, j * 512:(j + 1) * 512], start=True, stop=True))
        pBc = ptok()
        op("vector", lambda v: v.tensor_copy(rstd_rep[:, :], psA[:, :]), waits=[pBc], inc=("V", 1))
        vRstdRep = op("vector", lambda v: v.tensor_copy(nm_rep[:, :], psB[:, :]), inc=("V", 1))
        vNmRep = vRstdRep

        DCO = [6, 7, 8, 9, 10, 11, 12, 13, 14, 15, 0, 1, 2, 3, 4, 5]
        applyTok = {}
        for k, dc in enumerate(DCO):
            op("vector", lambda v, dc=dc: v.tensor_mul(tmpP1[:, :], XBF[:, dc, :], rstd_rep[:, :]),
               waits=([pBc] if k == 0 else []), inc=("V", 1))
            applyTok[dc] = op("vector", lambda v, dc=dc: v.tensor_add(
                xn_ap(dc)[:, :], tmpP1[:, :], nm_rep[:, :]), inc=("V", 1))
        vXnDone = tok("V")

        if debug:
            op("gpsimd", lambda g: g.dma_start(out=dbg_xn, in_=Abf[:, :]), waits=[vXnDone], inc=("W", 16))

        qkv_sb = bview(Bbf, 0, 6 * T).rearrange("p (fc t) -> p fc t", t=T)
        qkvw = [bview(Bbf, 24576, 12288).rearrange("p (fc dc f) -> p fc dc f", fc=6, dc=16),
                bview(Bbf, 49152, 12288).rearrange("p (fc dc f) -> p fc dc f", fc=6, dc=16)]
        out_loc = bview(Bbf, 73728, 2 * T).rearrange("p (hh t) -> p hh t", t=T)
        qt_buf = bview(Bbf, 81920, 2 * T).rearrange("p (hh t) -> p hh t", t=T)

        def psO(hh):
            return psB[:, hh * 1024: hh * 1024 + 256]

        def psKVd(hh):
            return psB[:, hh * 1024 + 256: hh * 1024 + 384]

        def psQK(hh):
            return psB[:, hh * 1024 + 512: hh * 1024 + 1024]

        def psQKbf(hh):
            return psB.ap()[:, hh * 1024 + 512: hh * 1024 + 1024].bitcast(BF16)

        wgbuf = [bview(Bbf, 90112, 2048).rearrange("p (dc f) -> p dc f", dc=16),
                 bview(Bbf, 94208, 2048).rearrange("p (dc f) -> p dc f", dc=16)]
        gatebuf = [bview(Bbf, 98304, T), bview(Bbf, 102400, T)]
        wgDmaT = {}
        gStop = {}
        gStore = {}
        gEvacH2 = {}

        projUseTok = {}
        regTok = {0: None, 1: None}
        lastScanPE = None
        lastQtilV = None
        lastOevac = {0: None, 1: None}
        lastQkd = {0: None, 1: None}
        lastIntra = {0: None, 1: None}
        lastCast = {0: None, 1: None}
        kvStoreTok = []
        outStoreTok = None
        qtStoreTok = None

        vRt = None
        wDmaPend = {}
        projPend = {}
        for g in range(NG):
            if g == 1:
                w0 = [vXnDone, lastOevac[0], lastOevac[1], lastQkd[0], lastQkd[1],
                      lastCast[0], lastCast[1]]
                for j in range(4):
                    for dc in range(16):
                        op("tensor", lambda t, j=j, dc=dc: t.matmul(
                            psB[0:1, j * 512:(j + 1) * 512], ones_c[:, :],
                            xn_ap(dc)[:, j * 512:(j + 1) * 512],
                            start=(dc == 0), stop=(dc == 15)),
                            waits=(w0 if (j == 0 and dc == 0) else []))
                pRt = ptok()
                vRt = op("vector", lambda v: v.tensor_copy(rt_bf[:, :], psB[0:1, :]), waits=[pRt], inc=("V", 1))
                if debug:
                    op("gpsimd", lambda g_: g_.dma_start(out=dbg_rt, in_=rt_bf[:, :]), waits=[vRt], inc=("W", 16))
            par = g % 2
            if g <= 1:
                wDma = []
                for i in range(3):
                    if g == 0:
                        w = [applyTok[min(15, 7 + 2 * i)]]
                    else:
                        w = [applyTok[min(15, 13 + 2 * i)]] if i < 2 else [applyTok[5], statsTok[15]]
                    wDma.append(op("gpsimd", lambda gp, g=g, i=i, par=par: gp.dma_start(
                        out=qkvw[par][:, 2 * i:2 * i + 2, :, :],
                        in_=wqkv[g].rearrange("p (fc dc f) -> p fc dc f", fc=6, dc=16)[:, 2 * i:2 * i + 2, :, :]),
                        waits=w, inc=("W", 16)))
                wDmaPend[g] = wDma
            else:
                wDma = wDmaPend[g]

            decTok = {}
            for hh in range(2):
                h = 2 * g + hh
                op("scalar", lambda s_, hh=hh, h=h: s_.activation(
                    qdec[:, hh, :], arr_sb[:, :], AF.Exp, scale=neg_s[:, h:h + 1]),
                    waits=([lConst, vSrep, sConsts] if hh == 0 else []), inc=("S", 1))
                op("scalar", lambda s_, hh=hh, h=h: s_.activation(
                    diagT[:, hh, :, :], idxT_sb[:, :, :], AF.Exp, scale=neg_s[:, h:h + 1]), inc=("S", 1))
                decTok[hh] = op("scalar", lambda s_, hh=hh, h=h: s_.activation(
                    kdecrep[:, hh, :, :], karr_sb[:, :, :], AF.Exp, scale=neg_s[:, h:h + 1]), inc=("S", 1))

            evacTok = {}

            def emit_unit_mms(gu, fc, half, first_waits):
                ru = (fc * 2 + half) % 2
                paru = gu % 2
                dcs = DCO if (gu == 0 and fc == 0 and half == 0) else list(range(16))
                for kdc, dc in enumerate(dcs):
                    for j in range(2):
                        ww = list(first_waits) if (kdc == 0 and j == 0) else []
                        if gu == 0 and fc == 0 and half == 0 and j == 0:
                            ww = ww + [applyTok[dc]]
                            if kdc == 0:
                                ww = ww + [vRstdRep]
                        op("tensor", lambda t, paru=paru, fc=fc, half=half, dc=dc, j=j, ru=ru,
                           kdc=kdc: t.matmul(
                            psA[:, ru * 1024 + j * 512: ru * 1024 + (j + 1) * 512],
                            qkvw[paru][:, fc, dc, :],
                            xn_ap(dc)[:, half * 1024 + j * 512: half * 1024 + (j + 1) * 512],
                            start=(kdc == 0), stop=(kdc == 15)),
                            waits=ww)
                return ptok()

            def emit_unit_evac(fc, half, pStop, extra=()):
                r2 = (fc * 2 + half) % 2
                we = [pStop] + list(extra)
                evacTok[(fc, half)] = op("scalar", lambda s_, fc=fc, half=half, r2=r2: s_.activation(
                    qkv_sb[:, fc, half * 1024:(half + 1) * 1024], psA[:, r2 * 1024:(r2 + 1) * 1024], AF.Silu),
                    waits=we, inc=("S", 1))
                regTok[r2] = evacTok[(fc, half)]

            if g == 0:
                units = [(fc, half) for fc in range(6) for half in range(2)]
                for (fc, half) in units:
                    pStop = emit_unit_mms(0, fc, half, [wDma[fc // 2], regTok[(fc * 2 + half) % 2]])
                    emit_unit_evac(fc, half, pStop)
            else:
                emit_unit_evac(0, 0, projPend[g][0], [lastScanPE, lastQtilV])
                emit_unit_evac(0, 1, projPend[g][1])
                for fc in range(1, 6):
                    for half in range(2):
                        pStop = emit_unit_mms(g, fc, half, [wDma[fc // 2], regTok[(fc * 2 + half) % 2]])
                        emit_unit_evac(fc, half, pStop)
            evacQ = {hh: evacTok[(hh, 1)] for hh in range(2)}
            evacK = {hh: evacTok[(2 + hh, 1)] for hh in range(2)}
            evacV = {hh: evacTok[(4 + hh, 1)] for hh in range(2)}

            class _Filler:
                def __init__(self):
                    self.th = []
                    self.i = 0

                def add(self, fn):
                    self.th.append(fn)

                def pump(self, n):
                    stop = min(self.i + n, len(self.th))
                    while self.i < stop:
                        self.th[self.i]()
                        self.i += 1

                def flush(self):
                    self.pump(len(self.th))

            filler = _Filler()
            for oc in (2 * g, 2 * g + 1):
                def _dma(oc=oc):
                    wgDmaT[oc] = op("gpsimd", lambda gp, oc=oc: gp.dma_start(
                        out=wgbuf[oc % 2][:, :, :], in_=wg[oc].rearrange("p (dc f) -> p dc f", dc=16)),
                        waits=([applyTok[DCO[-1]]] if oc < 2 else [gStop[oc - 2]]), inc=("W", 16))
                filler.add(_dma)
                for r in range(2):
                    for dc in range(16):
                        def _mm(oc=oc, r=r, dc=dc):
                            w = [wgDmaT[oc], regTok[r]] if dc == 0 else []
                            for j in range(2):
                                op("tensor", lambda t, oc=oc, dc=dc, j=j, r=r: t.matmul(
                                    psA[:, r * 1024 + j * 512: r * 1024 + (j + 1) * 512],
                                    wgbuf[oc % 2][:, dc, :],
                                    xn_ap(dc)[:, r * 1024 + j * 512: r * 1024 + (j + 1) * 512],
                                    start=(dc == 0), stop=(dc == 15)),
                                    waits=(w if j == 0 else []))
                        filler.add(_mm)

                    def _evac(oc=oc, r=r):
                        pG = ptok()
                        if r == 1:
                            gStop[oc] = pG
                        we = [pG]
                        if r == 0 and oc >= 2:
                            we.append(gStore[oc - 2])
                        regTok[r] = op("scalar", lambda s_, oc=oc, r=r: s_.activation(
                            gatebuf[oc % 2][:, r * 1024:(r + 1) * 1024],
                            psA[:, r * 1024:(r + 1) * 1024], AF.Sigmoid), waits=we, inc=("S", 1))
                        gEvacH2[(oc, r)] = regTok[r]
                    filler.add(_evac)

                def _store(oc=oc):
                    gStore[oc] = op("sync", lambda s_, oc=oc: s_.dma_start(
                        out=gate_sc[oc * 128:(oc + 1) * 128, :], in_=gatebuf[oc % 2][:, :]),
                        waits=[gEvacH2[(oc, 0)], gEvacH2[(oc, 1)], gStore.get(oc - 2)], inc=("L", 16))
                filler.add(_store)

            if g + 1 < NG:
                gn = g + 1
                parn = gn % 2
                wDmaPend[gn] = []

                def _wdma_n(i, gn=gn, parn=parn):
                    w = [projUseTok.get(gn - 2)] if i == 0 else []
                    wDmaPend[gn].append(op("gpsimd", lambda gp, gn=gn, i=i, parn=parn: gp.dma_start(
                        out=qkvw[parn][:, 2 * i:2 * i + 2, :, :],
                        in_=wqkv[gn].rearrange("p (fc dc f) -> p fc dc f", fc=6, dc=16)[:, 2 * i:2 * i + 2, :, :]),
                        waits=w, inc=("W", 16)))
                for i in range(3):
                    filler.add(lambda i=i: _wdma_n(i))
                projPend[gn] = {}
                for ui, half in ((0, 0), (1, 1)):
                    for kq in range(4):
                        def _pu(gn=gn, half=half, ui=ui, kq=kq):
                            ru = half
                            parn2 = gn % 2
                            for kdc in range(4 * kq, 4 * kq + 4):
                                for j in range(2):
                                    ww = ([wDmaPend[gn][0], regTok[ru]]
                                          if (kdc == 0 and j == 0) else [])
                                    op("tensor", lambda t, parn2=parn2, half=half, kdc=kdc, j=j,
                                       ru=ru: t.matmul(
                                        psA[:, ru * 1024 + j * 512: ru * 1024 + (j + 1) * 512],
                                        qkvw[parn2][:, 0, kdc, :],
                                        xn_ap(kdc)[:, half * 1024 + j * 512: half * 1024 + (j + 1) * 512],
                                        start=(kdc == 0), stop=(kdc == 15)),
                                        waits=ww)
                            if kq == 3:
                                projPend[gn][ui] = ptok()
                        filler.add(_pu)

            for b in range(NB):
                c0 = b * BLK
                vq = {}
                vev = {}
                vKtl = {}
                pQk = {}
                pKvd = {}
                for hh in range(2):
                    qs = qkv_sb[:, hh, c0:c0 + BLK]
                    ks = qkv_sb[:, 2 + hh, c0:c0 + BLK]
                    vq[hh] = op("vector", lambda v, hh=hh, qs=qs, c0=c0: v.tensor_mul(
                        qt_buf[:, hh, c0:c0 + BLK], qs, qdec[:, hh, :]),
                        waits=[evacQ[hh], decTok[hh], lastIntra[hh],
                               (qtStoreTok if b == 0 else None)], inc=("V", 1))
                    w = [evacK[hh], evacV[hh], lConst, lastQkd[hh], lastIntra[hh]]
                    if g == 0 and b == 0:
                        w.append(vNmRep)
                    if g == 1 and b == 0:
                        w.append(vRt)
                    for nh in range(2):
                        op("tensor", lambda t, hh=hh, ks=ks, nh=nh: t.transpose(
                            psQKbf(hh)[:, nh * 128:(nh + 1) * 128], ks[:, nh * 128:(nh + 1) * 128], iden[:, :]),
                            waits=(w if nh == 0 else []))
                    for nh in range(2):
                        op("tensor", lambda t, hh=hh, c0=c0, nh=nh: t.transpose(
                            psQKbf(hh)[:, 256 + nh * 128: 256 + (nh + 1) * 128],
                            qkv_sb[:, 4 + hh, c0 + nh * 128: c0 + (nh + 1) * 128], iden[:, :]))
                    pTr = ptok()
                    vev[hh] = op("vector", lambda v, hh=hh: v.tensor_copy(
                        kvtok[:, hh, :, :].rearrange("p a b -> p (a b)"), psQKbf(hh)[:, 0:512]),
                        waits=[pTr], inc=("V", 1))
                    op("vector", lambda v, hh=hh: v.tensor_mul(
                        ktl[:, hh, :, :].rearrange("p a b -> p (a b)"),
                        kvtok[:, hh, 0:2, :].rearrange("p a b -> p (a b)"),
                        kdecrep[:, hh, :, :].rearrange("p a b -> p (a b)")), inc=("V", 1))
                    vKtl[hh] = tok("V")
                filler.pump(1)
                for hh in range(2):
                    w = [vKtl[hh], lastOevac[hh], lastCast[hh]]
                    if b == 0:
                        w.append(kvStoreTok[-1] if kvStoreTok else None)
                    for nh in range(2):
                        op("tensor", lambda t, hh=hh, nh=nh: t.matmul(
                            psKVd(hh), ktl[:, hh, nh, :], kvtok[:, hh, 2 + nh, :],
                            start=(nh == 0), stop=(nh == 1)),
                            waits=(w if nh == 0 else []))
                    pKvd[hh] = ptok()
                for hh in range(2):
                    if b > 0:
                        op("tensor", lambda t, hh=hh, c0=c0: t.matmul(
                            psO(hh), kvbf[:, hh, :], qt_buf[:, hh, c0:c0 + BLK], start=True, stop=False),
                            waits=[lastCast[hh], vq[hh]])
                for hh in range(2):
                    qs = qkv_sb[:, hh, c0:c0 + BLK]
                    ks = qkv_sb[:, 2 + hh, c0:c0 + BLK]
                    for nh in range(2):
                        op("tensor", lambda t, hh=hh, ks=ks, qs=qs, nh=nh: t.matmul(
                            psQK(hh)[:, nh * 256:(nh + 1) * 256], ks[:, nh * 128:(nh + 1) * 128], qs,
                            start=True, stop=True),
                            waits=([vev[hh], vq[hh]] if nh == 0 else []))
                    pQk[hh] = ptok()
                    op("vector", lambda v, hh=hh: v.tensor_tensor(
                        qkd[:, hh, :, :].rearrange("p a b -> p (a b)"), psQK(hh)[:, 0:512],
                        diagT[:, hh, :, :].rearrange("p a b -> p (a b)"),
                        ALU.mult), waits=[pQk[hh]], inc=("V", 1))
                    lastQkd[hh] = tok("V")
                filler.pump(3)
                for hh in range(2):
                    h = 2 * g + hh
                    for nh in range(2):
                        op("tensor", lambda t, hh=hh, nh=nh, b=b: t.matmul(
                            psO(hh), kvtok[:, hh, 2 + nh, :], qkd[:, hh, nh, :],
                            start=(b == 0 and nh == 0), stop=(nh == 1)),
                            waits=([lastQkd[hh]] if nh == 0 else []))
                    lastIntra[hh] = ptok()
                    lastScanPE = lastIntra[hh]
                    lastOevac[hh] = op("scalar", lambda s_, hh=hh, c0=c0: s_.activation(
                        out_loc[:, hh, c0:c0 + BLK], psO(hh), AF.Copy),
                        waits=[lastIntra[hh], (outStoreTok if b == 0 else None)], inc=("S", 1))
                    if b == 0:
                        op("vector", lambda v, hh=hh: v.tensor_copy(kvf[:, hh, :], psKVd(hh)),
                           waits=[lastIntra[hh], lastOevac[hh], (kvStoreTok[-1] if kvStoreTok else None)],
                           inc=("V", 1))
                    else:
                        op("vector", lambda v, hh=hh, h=h: v.scalar_tensor_tensor(
                            kvf[:, hh, :], kvf[:, hh, :], bd_all[:, h:h + 1], psKVd(hh), ALU.mult, ALU.add),
                            waits=[lastIntra[hh], lastOevac[hh], sConsts], inc=("V", 1))
                    vStt = tok("V")
                    lastCast[hh] = op("scalar", lambda s_, hh=hh: s_.activation(
                        kvbf[:, hh, :], kvf[:, hh, :], AF.Copy), waits=[vStt], inc=("S", 1))
                lastQtilV = vq[1]
                filler.pump(2)

            filler.flush()
            outStoreTok = op("sync", lambda s_, g=g: s_.dma_start(
                out=out_sc[2 * g * 128:(2 * g + 2) * 128, :].rearrange("(hh p) t -> p hh t", p=128),
                in_=out_loc[:, :, :]), waits=[lastOevac[0], lastOevac[1], outStoreTok], inc=("L", 16))
            qtStoreTok = op("sync", lambda s_, g=g: s_.dma_start(
                out=q_sc[2 * g * 128:(2 * g + 2) * 128, :].rearrange("(hh p) t -> p hh t", p=128),
                in_=qt_buf[:, :, :]), waits=[lastQtilV, qtStoreTok], inc=("L", 16))
            kvStoreTok.append(op("sync", lambda s_, g=g: s_.dma_start(
                out=kv_local.ap()[2 * g * 128:(2 * g + 2) * 128, :].rearrange("(hh p) e -> p hh e", p=128),
                in_=kvf[:, :, :]), waits=[lastCast[0], lastCast[1]], inc=("L", 16)))
            projUseTok[g] = lastScanPE

        op("gpsimd", lambda g: g.collective_compute(
            "AllGather", ALU.bypass, replica_groups=[list(range(NCORE))],
            ins=[kv_local.ap().opt()], outs=[kv_all.ap().opt()]),
            waits=[kvStoreTok[-1]], inc=("C", 1))
        ccTok = tok("C")

        p3Done = [x for x in (lastScanPE, lastQtilV, outStoreTok, qtStoreTok, kvStoreTok[-1],
                              lastOevac[0], lastOevac[1], lastQkd[0], lastQkd[1],
                              lastCast[0], lastCast[1], regTok[0], regTok[1],
                              gStore[14], gStore[15], gStop[15]) if x is not None]
        p15Done = p3Done

        h_sb = bview(Bbf, 0, HCN * TFF).rearrange("p (hc t) -> p hc t", t=TFF)
        wgu = {"g": bview(Bbf, 90112, 2048).rearrange("p (dc f) -> p dc f", dc=16),
               "u": bview(Bbf, 94208, 2048).rearrange("p (dc f) -> p dc f", dc=16)}
        wd_b = bview(Bbf, 98304, HCN * 128).rearrange("p (hc f) -> p hc f", hc=HCN)
        gatebf = bview(Bbf, 109568, TFF)
        upbf = bview(Bbf, 111616, TFF)
        kvin32 = bview(Bf, 113664, NH * HD).rearrange("p (h e) -> p h e", h=NH)
        kvinbf = bview(Bbf, 98304, NH * HD).rearrange("p (h e) -> p h e", h=NH)

        mmTok = {}
        evTok = {}
        hTok = {}
        dStopAll = {}
        dEvacAll = {}
        gscStore = {}
        dHalf = {}
        for tt in range(2):
            t0 = tt * TFF
            for hc in range(HCN):
                for mat, srcw, scl in (("g", fg, sg_sb), ("u", fu, su_sb)):
                    w = []
                    if (tt, hc - 1, mat) in mmTok:
                        w.append(mmTok[(tt, hc - 1, mat)])
                    elif tt > 0:
                        w.append(mmTok[(tt - 1, HCN - 1, mat)])
                    else:
                        w += p15Done
                    dmaT = op("gpsimd", lambda gp, hc=hc, srcw=srcw, mat=mat: gp.dma_start(
                        out=wgu[mat][:, :, :], in_=srcw[hc].rearrange("p (dc f) -> p dc f", dc=16)),
                        waits=w, inc=("W", 16))
                    roff = 0 if mat == "g" else 1024
                    w = [dmaT, vXnDone, vRt]
                    if (tt, hc - 1, mat) in evTok:
                        w.append(evTok[(tt, hc - 1, mat)])
                    elif tt > 0:
                        w.append(evTok[(tt - 1, HCN - 1, mat)])
                    else:
                        w += p15Done
                    for dc in range(16):
                        for j in range(2):
                            op("tensor", lambda t, hc=hc, mat=mat, dc=dc, j=j, roff=roff, t0=t0: t.matmul(
                                psA[:, roff + j * 512: roff + (j + 1) * 512],
                                wgu[mat][:, dc, :],
                                xn_ap(dc)[:, t0 + j * 512: t0 + (j + 1) * 512],
                                start=(dc == 0), stop=False),
                                waits=(w if (dc == 0 and j == 0) else []))
                    for j in range(2):
                        op("tensor", lambda t, j=j, roff=roff, t0=t0: t.matmul(
                            psA[:, roff + j * 512: roff + (j + 1) * 512], n128_r[:, :],
                            rt_bf[0:1, t0 + j * 512: t0 + (j + 1) * 512], start=False, stop=True))
                    mmTok[(tt, hc, mat)] = ptok()
                    w = [mmTok[(tt, hc, mat)], lConst]
                    if (tt, hc - 1) in hTok:
                        w.append(hTok[(tt, hc - 1)])
                    elif tt > 0:
                        w.append(hTok[(tt - 1, HCN - 1)])
                        w.append(gscStore[(tt - 1, 14)])
                        w.append(gscStore[(tt - 1, 15)])
                    func = AF.Silu if mat == "g" else AF.Copy
                    dst = gatebf if mat == "g" else upbf
                    evTok[(tt, hc, mat)] = op("scalar", lambda s_, hc=hc, scl=scl, roff=roff,
                                              func=func, dst=dst: s_.activation(
                        dst[:, :], psA[:, roff:roff + TFF], func, scale=scl[:, hc:hc + 1]),
                        waits=w, inc=("S", 1))
                hTok[(tt, hc)] = op("vector", lambda v, hc=hc: v.tensor_mul(
                    h_sb[:, hc, :], gatebf[:, :], upbf[:, :]),
                    waits=[evTok[(tt, hc, "g")], evTok[(tt, hc, "u")]], inc=("V", 1))
                if tt == 1 and exq:
                    exq.pop(0)()
            hAll = hTok[(tt, HCN - 1)]

            for j in range(2):
                for hc in range(HCN):
                    op("tensor", lambda t, j=j, hc=hc: t.matmul(
                        psA[0:1, j * 512:(j + 1) * 512], ones_c[:, :], h_sb[:, hc, j * 512:(j + 1) * 512],
                        start=(hc == 0), stop=(hc == HCN - 1)),
                        waits=([hAll, evTok[(tt, HCN - 1, "g")], evTok[(tt, HCN - 1, "u")]]
                               if (j == 0 and hc == 0) else []))
            pRh = ptok()
            vRh = op("vector", lambda v: v.tensor_copy(rh_bf[:, :], psA[0:1, 0:TFF]), waits=[pRh], inc=("V", 1))

            dStop = {}
            dEvac = {}
            HSPL = 22
            for fc in range(16):
                wA = []
                if (tt, fc - 1) in dHalf:
                    wA.append(dHalf[(tt, fc - 1)])
                elif tt > 0:
                    wA.append(dHalf[(tt - 1, 15)])
                else:
                    wA += p15Done
                fdA = op("gpsimd", lambda gp, fc=fc: gp.dma_start(
                    out=wd_b[:, 0:HSPL, :], in_=fd[fc].rearrange("p (hc f) -> p hc f", hc=HCN)[:, 0:HSPL, :]),
                    waits=wA, inc=("W", 16))
                wB = []
                if fc >= 1:
                    wB.append(dStop[fc - 1])
                elif tt > 0:
                    wB.append(dStopAll[(tt - 1, 15)])
                else:
                    wB += p15Done
                fdB = op("gpsimd", lambda gp, fc=fc: gp.dma_start(
                    out=wd_b[:, HSPL:HCN, :], in_=fd[fc].rearrange("p (hc f) -> p hc f", hc=HCN)[:, HSPL:HCN, :]),
                    waits=wB, inc=("W", 16))
                reg_off = (fc % 2) * 1024
                w = [fdA, hAll, vRh]
                if fc >= 2:
                    w.append(dEvac[fc - 2])
                elif tt > 0:
                    w.append(dEvacAll[(tt - 1, 14 + fc)])
                for hc in range(HCN):
                    for j in range(2):
                        ww = w if (hc == 0 and j == 0) else ([fdB] if (hc == HSPL and j == 0) else [])
                        op("tensor", lambda t, fc=fc, hc=hc, j=j, reg_off=reg_off: t.matmul(
                            psB[:, reg_off + j * 512: reg_off + (j + 1) * 512],
                            wd_b[:, hc, :], h_sb[:, hc, j * 512:(j + 1) * 512],
                            start=(hc == 0), stop=False),
                            waits=ww)
                    if hc == HSPL - 1:
                        dHalf[(tt, fc)] = ptok()
                for j in range(2):
                    op("tensor", lambda t, j=j, reg_off=reg_off: t.matmul(
                        psB[:, reg_off + j * 512: reg_off + (j + 1) * 512], n128_r[:, :],
                        rh_bf[0:1, j * 512:(j + 1) * 512], start=False, stop=True))
                dStop[fc] = ptok()
                dStopAll[(tt, fc)] = dStop[fc]
                w = [dStop[fc]]
                if fc >= 2:
                    w.append(gscStore[(tt, fc - 2)])
                elif tt > 0:
                    w.append(gscStore[(tt - 1, 14 + fc)])
                dbuf = gatebf if fc % 2 == 0 else upbf
                dEvac[fc] = op("scalar", lambda s_, fc=fc, reg_off=reg_off, dbuf=dbuf: s_.activation(
                    dbuf[:, :], psB[:, reg_off:reg_off + TFF], AF.Silu, scale=sd_sb[:, fc:fc + 1]),
                    waits=w, inc=("S", 1))
                dEvacAll[(tt, fc)] = dEvac[fc]
                gscStore[(tt, fc)] = op("sync", lambda s_, fc=fc, t0=t0, dbuf=dbuf: s_.dma_start(
                    out=g_sc[fc * 128:(fc + 1) * 128, t0:t0 + TFF], in_=dbuf[:, :]),
                    waits=[dEvac[fc]], inc=("L", 16))
            if tt == 0:
                for h in range(NH):
                    op("scalar", lambda s_, h=h: s_.activation(
                        scx[:, h, :], mrep[:, :], AF.Exp, scale=neg_s[:, h:h + 1]),
                        waits=([lConst, vSrep] if h == 0 else []), inc=("S", 1))
                sScx = tok("S")
                exq = []
                exstate = {"stt": None}

                def _expkt(cp, hq):
                    w = [ccTok, exstate["stt"]]
                    kvLoad = op("sync", lambda s_, cp=cp, hq=hq: s_.dma_start(
                        out=kvc[:, :, :],
                        in_=kv_all.ap()[cp * 2048 + hq * 512: cp * 2048 + (hq + 1) * 512, :]
                        .rearrange("(h p) e -> p h e", p=128)),
                        waits=w, inc=("L", 16))
                    for h4 in range(4):
                        h = hq * 4 + h4
                        if cp == 0:
                            op("vector", lambda v, h=h, h4=h4, cp=cp: v.tensor_scalar_mul(
                                kvin32[:, h, :], kvc[:, h4, :], scx[:, h, cp:cp + 1]),
                                waits=([kvLoad, sScx] if h4 == 0 else []), inc=("V", 1))
                        else:
                            op("vector", lambda v, h=h, h4=h4, cp=cp: v.scalar_tensor_tensor(
                                kvin32[:, h, :], kvc[:, h4, :], scx[:, h, cp:cp + 1],
                                kvin32[:, h, :], ALU.mult, ALU.add),
                                waits=([kvLoad] if h4 == 0 else []), inc=("V", 1))
                    exstate["stt"] = tok("V")

                for cp in range(NCORE):
                    for hq in range(4):
                        exq.append((lambda cp=cp, hq=hq: _expkt(cp, hq)))
        while exq:
            exq.pop(0)()
        vKvinSum = exstate["stt"]
        vKvin = None
        p2Done = [dStopAll[(1, 15)], dEvacAll[(1, 15)], gscStore[(1, 14)], gscStore[(1, 15)],
                  hTok[(1, HCN - 1)], mmTok[(1, HCN - 1, "g")], mmTok[(1, HCN - 1, "u")],
                  evTok[(1, HCN - 1, "g")], evTok[(1, HCN - 1, "u")]]

        vKvin = op("vector", lambda v: v.tensor_copy(
            kvinbf[:, :, :].rearrange("p a b -> p (a b)"),
            kvin32[:, :, :].rearrange("p a b -> p (a b)")), waits=[vKvinSum] + list(p2Done), inc=("V", 1))
        if debug:
            op("gpsimd", lambda g: g.dma_start(
                out=dbg_kvin, in_=kvin32[:, :, :].rearrange("p a b -> p (a b)")),
                waits=[vKvin], inc=("W", 16))

        olc = [bview(Bbf, 32768, NH * 256).rearrange("p (h t) -> p h t", h=NH),
               bview(Bbf, 40960, NH * 256).rearrange("p (h t) -> p h t", h=NH)]
        qts = [bview(Bbf, 49152, NH * 256).rearrange("p (h t) -> p h t", h=NH),
               bview(Bbf, 57344, NH * 256).rearrange("p (h t) -> p h t", h=NH)]
        gts = [bview(Bbf, 65536, NH * 256).rearrange("p (h t) -> p h t", h=NH),
               bview(Bbf, 73728, NH * 256).rearrange("p (h t) -> p h t", h=NH)]
        oco = [bview(Bbf, 81920, NH * 256).rearrange("p (h t) -> p h t", h=NH),
               bview(Bbf, 90112, NH * 256).rearrange("p (h t) -> p h t", h=NH)]
        row_ss5 = [bview(Bf, 102400, 256)[0:1, :], bview(Bf, 103424, 256)[0:1, :]]
        row_rstd5 = [bview(Bbf, 104448, 256)[0:1, :], bview(Bbf, 104960, 256)[0:1, :]]
        sqt = [bview(Bbf, 105472, 256), bview(Bbf, 105984, 256),
               bview(Bbf, 114688, 256), bview(Bbf, 115200, 256)]
        utmp = [bview(Bf, 106496, 256), bview(Bf, 107520, 256)]
        nwrow = bview(Bbf, 108544, D)[0:1, :]

        nwTok = op("gpsimd", lambda g: g.dma_start(out=nwrow[:, :], in_=nw), waits=list(p2Done), inc=("W", 16))

        NSL = 8
        sl_u2last = {}
        sl_sqlast = {}
        sl_corrPE = {}
        sl_u1last = {}
        sl_sss = {}
        sl_vrec = {}
        icorr = 0
        corrV = {}
        prevOuterPE = {}
        for sl in range(NSL):
            par = sl % 2
            qc = sl * 256
            ldTok = []
            for dram, dst in ((out_sc, olc), (q_sc, qts), (gate_sc, gts)):
                for i in range(2):
                    w = []
                    if i == 0 and dram is out_sc:
                        if sl == 0:
                            w = list(p2Done) + [vKvin]
                        elif sl == 1:
                            w = [vKvin]
                        else:
                            w = [sl_u1last[sl - 2], sl_sqlast[sl - 2], sl_corrPE[sl - 2]]
                    ldTok.append(op("sync", lambda s_, dram=dram, dst=dst, qc=qc, i=i, par=par: s_.dma_start(
                        out=dst[par][:, 8 * i:8 * i + 8, :],
                        in_=dram[:, qc:qc + 256].rearrange("(h p) t -> p h t", p=128)[:, 8 * i:8 * i + 8, :]),
                        waits=w, inc=("L", 16)))

            for h in range(NH):
                roff = (icorr % 4) * 512
                w = [vKvin, ldTok, corrV.get(icorr - 4)]
                if sl == 0 and h < 4:
                    w += list(p2Done)
                op("tensor", lambda t, h=h, roff=roff, par=par: t.matmul(
                    psA[:, roff:roff + 256], kvinbf[:, h, :], qts[par][:, h, :],
                    start=True, stop=True), waits=w)
                pC = ptok()
                if h == NH - 1:
                    sl_corrPE[sl] = pC
                w = [pC, sConsts]
                if h == 0 and sl >= 2:
                    w.append(sl_sqlast[sl - 2])
                corrV[icorr] = op("vector", lambda v, h=h, sl=sl, par=par, roff=roff: v.scalar_tensor_tensor(
                    oco[par][:, h, :], psA[:, roff:roff + 256], bdg[:, sl, h:h + 1],
                    olc[par][:, h, :], ALU.mult, ALU.add), waits=w, inc=("V", 1))
                icorr += 1

            ssreg = psB[0:1, 0:256] if par == 0 else psB[0:1, 1536:1792]
            for h in range(NH):
                if h >= 4:
                    prd = prevOuterPE.get((sl, "sq", h - 4))
                else:
                    prd = prevOuterPE.get((sl - 1, "sq", NH - 4 + h))
                w = [corrV[icorr - NH + h], prd]
                w = [x for x in w if x is not None]
                sq = op("scalar", lambda s_, h=h, par=par: s_.activation(
                    sqt[h % 4][:, :], oco[par][:, h, :], AF.Square), waits=w, inc=("S", 1))
                if h == NH - 1:
                    sl_sqlast[sl] = sq
                ww = [sq]
                if h == 0 and sl >= 2:
                    ww.append(sl_sss[sl - 2])
                op("tensor", lambda t, h=h, ssreg=ssreg: t.matmul(
                    ssreg, ones_c[:, :], sqt[h % 4][:, :], start=(h == 0), stop=(h == NH - 1)),
                    waits=ww)
                prevOuterPE[(sl, "sq", h)] = ptok()
            sl_sss[sl] = op("scalar", lambda s_, ssreg=ssreg, par=par: s_.activation(
                row_ss5[par][:, :], ssreg, AF.Sqrt, scale=1.0 / D, bias=eps_rms[:, :]),
                waits=[prevOuterPE[(sl, "sq", NH - 1)]], inc=("S", 1))
            w = [sl_sss[sl]]
            if sl >= 2:
                w.append(sl_u2last_pe)
            op("vector", lambda v, par=par: v.reciprocal(utmp[0][0:1, :], row_ss5[par][:, :]),
               waits=w, inc=("V", 1))
            sl_vrec[sl] = op("vector", lambda v, par=par: v.tensor_copy(row_rstd5[par][:, :], utmp[0][0:1, :]),
                             inc=("V", 1))

            u2T = {}
            for oc in range(16):
                roff = 512 + (oc % 2) * 512
                w = [sl_vrec[sl], nwTok, u2T.get(oc - 2)]
                if oc < 2 and sl >= 1:
                    w.append(sl_u2last.get(sl - 1))
                op("tensor", lambda t, oc=oc, roff=roff, par=par: t.matmul(
                    psB[:, roff:roff + 256], nwrow[0:1, oc * 128:(oc + 1) * 128], row_rstd5[par][:, :],
                    start=True, stop=True), waits=w)
                pOut = ptok()
                u1 = op("vector", lambda v, oc=oc, par=par: v.tensor_mul(
                    utmp[oc % 2][:, :], oco[par][:, oc, :], gts[par][:, oc, :]),
                    waits=[corrV[icorr - NH + oc], ldTok[-1]], inc=("V", 1))
                u2T[oc] = op("vector", lambda v, oc=oc, qc=qc, roff=roff: v.tensor_mul(
                    u_ap(oc)[:, qc:qc + 256], utmp[oc % 2][:, :], psB[:, roff:roff + 256]),
                    waits=[pOut], inc=("V", 1))
            sl_u1last[sl] = u1
            sl_u2last[sl] = u2T[15]
            sl_u2last_pe = pOut
        uDone = sl_u2last[NSL - 1]

        wo_b = [bview(Bbf, 32768, 2048).rearrange("p (oc f) -> p oc f", oc=16),
                bview(Bbf, 36864, 2048).rearrange("p (oc f) -> p oc f", oc=16)]
        gbf6 = [bview(Bbf, 40960, T), bview(Bbf, 45056, T)]
        xc6 = [bview(Bf, 49152, T), bview(Bf, 57344, T)]
        osb = [bview(Bf, 65536, T), bview(Bf, 73728, T)]
        tmp6 = [bview(Bf, 81920, T), bview(Bf, 90112, T)]

        mmStop = {}
        p1Tok = {}
        p2Tok = {}
        stTok = {}
        for fc in range(16):
            w = [uDone] if fc < 2 else [mmStop[fc - 2]]
            woT = op("gpsimd", lambda gp, fc=fc: gp.dma_start(
                out=wo_b[fc % 2][:, :, :], in_=wo[fc].rearrange("p (oc f) -> p oc f", oc=16)),
                waits=w, inc=("W", 16))
            w = [uDone] if fc < 2 else [p1Tok[fc - 2]]
            gl = op("sync", lambda s_, fc=fc: s_.dma_start(
                out=gbf6[fc % 2][:, :], in_=g_sc[fc * 128:(fc + 1) * 128, :]), waits=w, inc=("L", 16))
            xl = op("sync", lambda s_, fc=fc: s_.dma_start(
                out=xc6[fc % 2][:, :], in_=xt[fc * 128:(fc + 1) * 128, :]),
                waits=([uDone] if fc < 2 else [p2Tok[fc - 2]]), inc=("L", 16))
            reg = psA if fc % 2 == 0 else psB
            w = [woT, uDone]
            if fc >= 2:
                w.append(p1Tok[fc - 2])
            for oc in range(16):
                for j in range(4):
                    op("tensor", lambda t, fc=fc, oc=oc, j=j, reg=reg: t.matmul(
                        reg[:, j * 512:(j + 1) * 512], wo_b[fc % 2][:, oc, :],
                        u_ap(oc)[:, j * 512:(j + 1) * 512], start=(oc == 0), stop=(oc == 15)),
                        waits=(w if (oc == 0 and j == 0) else [])),
            mmStop[fc] = ptok()
            p1Tok[fc] = op("vector", lambda v, fc=fc, reg=reg: v.tensor_mul(
                tmp6[fc % 2][:, :], reg[:, :], gbf6[fc % 2][:, :]), waits=[mmStop[fc], gl], inc=("V", 1))
            p2Tok[fc] = op("vector", lambda v, fc=fc: v.tensor_add(
                osb[fc % 2][:, :], tmp6[fc % 2][:, :], xc6[fc % 2][:, :]),
                waits=[xl, stTok.get(fc - 2)], inc=("V", 1))
            stTok[fc] = op("sync", lambda s_, fc=fc: s_.dma_start(
                out=out_d[fc * 128:(fc + 1) * 128, :], in_=osb[fc % 2][:, :]), waits=[p2Tok[fc]], inc=("O", 16))

        for (s, v) in qtoks("O"):
            op("sync", lambda s_, s0=s, v0=v: s_.wait_ge(sem[s0], v0))

        block = e(nc.Block())

        SELF_SER = {"vector": "V", "scalar": "S"}

        def run_engine(eng_obj, name):
            own = SELF_SER.get(name)
            nown = 0
            for (engname, ws, fn, inc) in ops:
                if engname != name:
                    continue
                if own is not None and nown > 0:
                    eng_obj.wait_ge(sem[own], nown)
                for (s, v) in ws:
                    eng_obj.wait_ge(sem[s], v)
                ins = fn(eng_obj)
                if inc is not None and ins is not None:
                    ins.then_inc(sem[inc[0]], inc[1])
                    if own is not None and inc[0] == own:
                        nown += inc[1]

        block.sync(lambda s_: run_engine(s_, "sync"))
        block.gpsimd(lambda g: run_engine(g, "gpsimd"))
        block.tensor(lambda t: run_engine(t, "tensor"))
        block.vector(lambda v: run_engine(v, "vector"))
        block.scalar(lambda s_: run_engine(s_, "scalar"))

    return nc


def _prep_inputs(inputs):
    x = np.asarray(inputs["L0_attn_hidden_states"])
    wqkv = np.asarray(inputs["L0_attn_w_qkv"])
    wgate = np.asarray(inputs["L0_attn_w_gate"])
    wout = np.asarray(inputs["L0_attn_w_out"])
    normw = np.asarray(inputs["L0_attn_norm_w"])
    slope = np.asarray(inputs["L0_attn_slope_rate"])
    fgq = np.asarray(inputs["L0_ffn_wq_gate"])
    fuq = np.asarray(inputs["L0_ffn_wq_up"])
    fdq = np.asarray(inputs["L0_ffn_wq_down"])
    sgv = np.asarray(inputs["L0_ffn_s_gate"])
    suv = np.asarray(inputs["L0_ffn_s_up"])
    sdv = np.asarray(inputs["L0_ffn_s_down"])

    xT = np.ascontiguousarray(x[0].T)

    wq3 = wqkv.reshape(16, 3, 128, 2048)
    blocks = []
    for g in range(8):
        rows = np.concatenate([wq3[2 * g, 0], wq3[2 * g + 1, 0],
                               wq3[2 * g, 1], wq3[2 * g + 1, 1],
                               wq3[2 * g, 2], wq3[2 * g + 1, 2]], axis=0)
        blocks.append(rows.reshape(6, 128, 16, 128).transpose(3, 0, 2, 1))
    wqkv_b = np.ascontiguousarray(np.stack(blocks)).reshape(8, 128, 6 * 16 * 128)

    wg_b = np.ascontiguousarray(
        wgate.reshape(16, 128, 16, 128).transpose(0, 3, 2, 1)).reshape(16, 128, 16 * 128)
    wo_b = np.ascontiguousarray(
        wout.reshape(16, 128, 16, 128).transpose(0, 3, 2, 1)).reshape(16, 128, 16 * 128)
    fg_b = np.ascontiguousarray(
        fgq.reshape(44, 128, 16, 128).transpose(0, 3, 2, 1)).reshape(44, 128, 16 * 128)
    fu_b = np.ascontiguousarray(
        fuq.reshape(44, 128, 16, 128).transpose(0, 3, 2, 1)).reshape(44, 128, 16 * 128)
    fd_b = np.ascontiguousarray(
        fdq.reshape(16, 128, 44, 128).transpose(0, 3, 2, 1)).reshape(16, 128, 44 * 128)

    sg_l = np.ascontiguousarray(sgv.reshape(44, 128).T)
    su_l = np.ascontiguousarray(suv.reshape(44, 128).T)
    sd_l = np.ascontiguousarray(sdv.reshape(16, 128).T)
    nw_l = np.ascontiguousarray(normw.reshape(1, 2048))
    sr_l = np.ascontiguousarray(slope.reshape(1, 16))

    in_maps = []
    for c in range(NCORE):
        m = np.full((1, NCORE), 1e9, dtype=np.float32)
        for cp in range(c):
            m[0, cp] = 2048.0 * (c - 1 - cp)
        in_maps.append(dict(
            xt=np.ascontiguousarray(xT[:, c * T:(c + 1) * T]),
            wqkv=wqkv_b, wg=wg_b, wo=wo_b, fg=fg_b, fu=fu_b, fd=fd_b,
            sg=sg_l, su=su_l, sd=sd_l, nw=nw_l, sr=sr_l, mrow=m))
    return in_maps


def kernel(**inputs):
    in_maps = _prep_inputs(inputs)
    nc = build(debug=False)
    res = run_bass_kernel_spmd(nc, in_maps, core_ids=list(range(NCORE)))
    outT = np.concatenate([r["out"] for r in res.results], axis=1)
    return np.ascontiguousarray(outT.T)[None].astype(np.float32)
